# version 03 (final)
# baseline (speedup 1.0000x reference)
import contextlib
import numpy as np
import ml_dtypes
import concourse.bass as bass
import concourse.mybir as mybir
from concourse.bass_utils import run_bass_kernel_spmd

F32 = mybir.dt.float32
BF16 = mybir.dt.bfloat16
AF = mybir.ActivationFunctionType
ALU = mybir.AluOpType
BF = ml_dtypes.bfloat16

NCORES = 8
B_LOC = 2
T = 2048
TOK = B_LOC * T
D = 1024
DI = 2048
DS = 16
DC = 4
DR = 64
E = 8
FF = 4096
NH = 16
HD = 64
NEG = -1e30
NCH = TOK // 512

DEBUG_TAPS = ()
DEBUG_LAUNCH = 1
PHASE_LIMIT = 99



def _t(a):
    return np.ascontiguousarray(np.asarray(a).T)


def prep_shared(inp):
    w = {}
    for d in ("f", "b"):
        w[f"ipT_{d}"] = _t(inp[f"L0_attn_in_proj_{d}"])
        w[f"xpT_{d}"] = _t(inp[f"L0_attn_x_proj_{d}"])
        w[f"dtwT_{d}"] = _t(inp[f"L0_attn_dt_w_{d}"])
        w[f"opT_{d}"] = _t(inp[f"L0_attn_out_proj_{d}"])
        w[f"convw_{d}"] = np.ascontiguousarray(
            np.asarray(inp[f"L0_attn_conv_w_{d}"]).reshape(16, 128, DC).transpose(1, 0, 2))
        w[f"convb_{d}"] = np.ascontiguousarray(np.asarray(inp[f"L0_attn_conv_b_{d}"]).reshape(16, 128).T)
        w[f"dtb_{d}"] = np.ascontiguousarray(np.asarray(inp[f"L0_attn_dt_b_{d}"]).reshape(16, 128).T)
        w[f"Alog_{d}"] = np.ascontiguousarray(
            np.asarray(inp[f"L0_attn_A_log_{d}"]).reshape(16, 128, DS).transpose(1, 0, 2))
        w[f"Dsk_{d}"] = np.ascontiguousarray(np.asarray(inp[f"L0_attn_D_{d}"]).reshape(16, 128).T)
    w["lnw"] = np.ascontiguousarray(np.asarray(inp["L0_attn_ln_w"]).reshape(8, 128).T)
    w["lnb"] = np.ascontiguousarray(np.asarray(inp["L0_attn_ln_b"]).reshape(8, 128).T)
    w["gwT"] = _t(inp["L0_ffn_gate_W"])
    w["w1e"] = np.ascontiguousarray(inp["L0_ffn_w1_e"])
    w["w3e"] = np.ascontiguousarray(inp["L0_ffn_w3_e"])
    w["w2e"] = np.ascontiguousarray(inp["L0_ffn_w2_e"])
    w["w1s"] = np.ascontiguousarray(np.concatenate(list(np.asarray(inp["L0_ffn_w1_s"])), axis=1))
    w["w3s"] = np.ascontiguousarray(np.concatenate(list(np.asarray(inp["L0_ffn_w3_s"])), axis=1))
    w["w2s"] = np.ascontiguousarray(np.concatenate(list(np.asarray(inp["L0_ffn_w2_s"])), axis=0))
    w["wqkv"] = np.ascontiguousarray(inp["L1_attn_Wqkv"])
    w["bqkv"] = np.ascontiguousarray(np.asarray(inp["L1_attn_bqkv"]).reshape(1, 3 * D))
    w["wproj"] = np.ascontiguousarray(inp["L1_attn_Wproj"])
    w["bproj"] = np.ascontiguousarray(np.asarray(inp["L1_attn_bproj"]).reshape(8, 128).T)
    w["ws"] = np.ascontiguousarray(inp["L1_ffn_Ws"])
    w["bs"] = np.ascontiguousarray(np.asarray(inp["L1_ffn_bs"]).reshape(E, 1))
    w["w1L1"] = np.ascontiguousarray(inp["L1_ffn_W1"])
    w["b1L1"] = np.ascontiguousarray(np.asarray(inp["L1_ffn_b1"]).reshape(E, 32, 128).transpose(0, 2, 1))
    w["w2L1"] = np.ascontiguousarray(inp["L1_ffn_W2"])
    w["b2L1"] = np.ascontiguousarray(np.asarray(inp["L1_ffn_b2"]).reshape(E, 8, 128).transpose(0, 2, 1))
    w["identf"] = np.eye(128, dtype=np.float32)
    w["identb"] = np.eye(128, dtype=BF)
    w["ones1x128"] = np.ones((1, 128), np.float32)
    w["ones128x1"] = np.ones((128, 1), np.float32)
    pos = np.arange(T, dtype=np.float64)
    inv = 1.0 / (10000.0 ** (np.arange(0, HD, 2, dtype=np.float64) / HD))
    fr = (pos[:, None] * inv[None, :]).astype(np.float32)
    w["cosT"] = np.cos(fr).astype(np.float32)
    w["sinT"] = np.sin(fr).astype(np.float32)
    cm = np.zeros((128, 128), np.float32)
    cm[np.triu_indices(128, 1)] = NEG
    w["causal"] = cm
    return w


def shard_input(x):
    return [_t(np.asarray(x[2 * c:2 * c + 2]).reshape(TOK, D)) for c in range(NCORES)]



class Ctx:
    def __init__(self, nc, st):
        self.nc = nc
        self.st = st
        self.sem = {}
        self.n = {}
        for name in ("d", "g", "y", "v", "s", "t", "p"):
            self.sem[name] = st.enter_context(nc.semaphore("sem_" + name))
            self.n[name] = 0

    def mk(self, *names):
        for name in names:
            if name not in self.sem:
                self.sem[name] = self.st.enter_context(self.nc.semaphore("sem_" + name))
                self.n[name] = 0

    def dma_on(self, inst, name):
        return self.inc(inst, name, 16)

    def vedge(self, v, inst):
        self.inc(inst, "v")
        v.wait_ge(self.sem["v"], self.n["v"])

    def inc(self, inst, name, amt=1):
        inst.then_inc(self.sem[name], amt)
        self.n[name] += amt
        return self.n[name]

    def dma(self, inst):
        return self.inc(inst, "d", 16)

    def gdma(self, inst):
        return self.inc(inst, "g", 16)

    def wait(self, eng, name, val=None):
        v = self.n[name] if val is None else val
        if v > 0:
            eng.wait_ge(self.sem[name], v)

    def barrier(self, block):
        @block.sync
        def _(sync):
            for nm in list(self.sem):
                if nm != "y":
                    self.wait(sync, nm)
            self.inc(sync.nop(), "y")
        yv = self.n["y"]

        for eng_name in ("vector", "scalar", "tensor", "gpsimd"):
            def mk(eng_name=eng_name):
                def fn(eng):
                    eng.wait_ge(self.sem["y"], yv)
                fn.__name__ = "_"
                return fn
            getattr(block, eng_name)(mk())


def sbuf(ctx, st, name, shape, dtype=F32):
    return st.enter_context(ctx.nc.sbuf_tensor(name, shape, dtype))


def psum(ctx, st, name, shape, dtype=F32):
    return st.enter_context(ctx.nc.psum_tensor(name, shape, dtype))


def bcast_rows(row_ap, parts=128):
    return row_ap.unsqueeze(0).broadcast_to([parts] + list(row_ap.shape))



def phase_ln(ctx, block, ap, src, dst32, dstb, wcol=None, bcol=None, tag=""):
    nd = D // 128
    with contextlib.ExitStack() as st:
        xt = [sbuf(ctx, st, f"ln{tag}_x{j}", [128, TOK], F32) for j in range(2)]
        sq = [sbuf(ctx, st, f"ln{tag}_sq{j}", [128, TOK], F32) for j in range(2)]
        onesb = sbuf(ctx, st, f"ln{tag}_ones", [128, 1], F32)
        stat = sbuf(ctx, st, f"ln{tag}_s1", [1, TOK], F32)
        stat2 = sbuf(ctx, st, f"ln{tag}_s2", [1, TOK], F32)
        mu = sbuf(ctx, st, f"ln{tag}_mu", [1, TOK], F32)
        rs = sbuf(ctx, st, f"ln{tag}_rs", [1, TOK], F32)
        MU = sbuf(ctx, st, f"ln{tag}_MU", [128, TOK], F32)
        RS = sbuf(ctx, st, f"ln{tag}_RS", [128, TOK], F32)
        ob = [sbuf(ctx, st, f"ln{tag}_ob{j}", [128, TOK], BF16) for j in range(2)]
        wc = sbuf(ctx, st, f"ln{tag}_wc", [128, nd], F32) if wcol else None
        bc = sbuf(ctx, st, f"ln{tag}_bc", [128, nd], F32) if bcol else None
        ps1 = psum(ctx, st, f"ln{tag}_ps1", [1, TOK])
        lA, lB = f"ln{tag}A", f"ln{tag}B"
        sA, sB = f"ln{tag}SA", f"ln{tag}SB"
        ctx.mk(lA, lB, sA, sB)
        lsem = [lA, lB]
        ssem_ = [sA, sB]

        @block.sync
        def _(sync):
            if wcol:
                ctx.dma(sync.dma_start(out=wc[:, :], in_=ap[wcol]))
                ctx.dma(sync.dma_start(out=bc[:, :], in_=ap[bcol]))
            ctx.dma(sync.dma_start(out=onesb[:, :], in_=ap["ones128x1"]))
        d0 = ctx.n["d"]

        t_base = ctx.n["t"]
        ld_t = {}

        @block.sync
        def _(sync):
            for i in range(nd):
                if i >= 2:
                    sync.wait_ge(ctx.sem["t"], t_base + i - 1)
                ctx.dma_on(sync.dma_start(out=xt[i % 2][:, :],
                                          in_=ap[src][i * 128:(i + 1) * 128, :]), lsem[i % 2])
                ld_t[i] = ctx.n[lsem[i % 2]]

        @block.tensor
        def _(t):
            t.wait_ge(ctx.sem["d"], d0)
            for i in range(nd):
                t.wait_ge(ctx.sem[lsem[i % 2]], ld_t[i])
                for c in range(NCH):
                    last = t.matmul(ps1[0:1, c * 512:(c + 1) * 512], onesb[:, 0:1],
                                    xt[i % 2][:, c * 512:(c + 1) * 512],
                                    start=(i == 0), stop=(i == nd - 1))
                ctx.inc(last, "t")
        t_sum = ctx.n["t"]

        @block.vector
        def _(v):
            v.wait_ge(ctx.sem["t"], t_sum)
            ctx.inc(v.tensor_copy(out=stat[:, :], in_=ps1[0:1, :]), "v")
        v_stat = ctx.n["v"]

        s_base = ctx.n["s"]
        t_base2 = ctx.n["t"]
        ld2_t = {}

        @block.sync
        def _(sync):
            for i in range(nd):
                if i >= 2:
                    sync.wait_ge(ctx.sem["s"], s_base + i - 1)
                else:
                    sync.wait_ge(ctx.sem["t"], t_base2)
                ctx.dma_on(sync.dma_start(out=xt[i % 2][:, :],
                                          in_=ap[src][i * 128:(i + 1) * 128, :]), lsem[i % 2])
                ld2_t[i] = ctx.n[lsem[i % 2]]

        @block.scalar
        def _(s):
            for i in range(nd):
                s.wait_ge(ctx.sem[lsem[i % 2]], ld2_t[i])
                if i >= 2:
                    s.wait_ge(ctx.sem["t"], t_base2 + i - 1)
                ctx.inc(s.activation(out=sq[i % 2][:, :], in_=xt[i % 2][:, :], func=AF.Square), "s")

        @block.tensor
        def _(t):
            t.wait_ge(ctx.sem["v"], v_stat)
            for i in range(nd):
                t.wait_ge(ctx.sem["s"], s_base + i + 1)
                for c in range(NCH):
                    last = t.matmul(ps1[0:1, c * 512:(c + 1) * 512], onesb[:, 0:1],
                                    sq[i % 2][:, c * 512:(c + 1) * 512],
                                    start=(i == 0), stop=(i == nd - 1))
                ctx.inc(last, "t")
        t_sq = ctx.n["t"]

        @block.vector
        def _(v):
            v.wait_ge(ctx.sem["t"], t_sq)
            v.tensor_copy(out=stat2[:, :], in_=ps1[0:1, :])
            v.tensor_scalar(out=mu[:, :], in0=stat[:, :], scalar1=1.0 / D, scalar2=None, op0=ALU.mult)
            v.tensor_scalar(out=stat2[:, :], in0=stat2[:, :], scalar1=1.0 / D, scalar2=None, op0=ALU.mult)
            v.tensor_tensor(out=rs[:, :], in0=mu[:, :], in1=mu[:, :], op=ALU.mult)
            v.tensor_tensor(out=rs[:, :], in0=stat2[:, :], in1=rs[:, :], op=ALU.subtract)
            ctx.inc(v.tensor_scalar(out=rs[:, :], in0=rs[:, :], scalar1=1e-5, scalar2=None, op0=ALU.add), "v")
        v_var = ctx.n["v"]

        @block.scalar
        def _(s):
            s.wait_ge(ctx.sem["v"], v_var)
            s.activation(out=rs[:, :], in_=rs[:, :], func=AF.Ln)
            ctx.inc(s.activation(out=rs[:, :], in_=rs[:, :], func=AF.Exp, scale=-0.5), "s")
        s_rs = ctx.n["s"]

        @block.sync
        def _(sync):
            sync.wait_ge(ctx.sem["s"], s_rs)
            ctx.dma(sync.dma_start(out=ap["rowbuf"][0:1, :], in_=mu[0:1, :]))
            ctx.dma(sync.dma_start(out=ap["rowbuf"][1:2, :], in_=rs[0:1, :]))
            ctx.wait(sync, "d")
            ctx.dma(sync.dma_start(out=MU[:, :],
                           in_=bcast_rows(ap["rowbuf"][0:1, :]).rearrange("p o n -> p (o n)")))
            ctx.dma(sync.dma_start(out=RS[:, :],
                                   in_=bcast_rows(ap["rowbuf"][1:2, :]).rearrange("p o n -> p (o n)")))
        d_bc = ctx.n["d"]

        v_base3 = ctx.n["v"]
        s_base3 = ctx.n["s"]
        ld3_t = {}
        store_d = {0: 0, 1: 0}

        for i in range(nd):
            j = i % 2

            @block.sync
            def _(sync, i=i, j=j):
                if i >= 2:
                    sync.wait_ge(ctx.sem["v"], v_base3 + i - 1)
                else:
                    sync.wait_ge(ctx.sem["s"], s_base3)
                ctx.dma_on(sync.dma_start(out=xt[i % 2][:, :],
                                          in_=ap[src][i * 128:(i + 1) * 128, :]), lsem[i % 2])
                ld3_t[i] = ctx.n[lsem[i % 2]]

            @block.vector
            def _(v, i=i, j=j, sd=store_d[j]):
                v.wait_ge(ctx.sem[lsem[j]], ld3_t[i])
                if i == 0:
                    v.wait_ge(ctx.sem["d"], d_bc)
                if sd:
                    v.wait_ge(ctx.sem[ssem_[j]], sd)
                v.tensor_tensor(out=sq[j][:, :], in0=xt[j][:, :], in1=MU[:, :], op=ALU.subtract)
                v.tensor_tensor(out=sq[j][:, :], in0=sq[j][:, :], in1=RS[:, :], op=ALU.mult)
                if wcol:
                    v.tensor_scalar(out=sq[j][:, :], in0=sq[j][:, :], scalar1=wc[:, i:i + 1],
                                    scalar2=bc[:, i:i + 1], op0=ALU.mult, op1=ALU.add)
                ctx.inc(v.tensor_copy(out=ob[j][:, :], in_=sq[j][:, :]), "v")
            vv = ctx.n["v"]

            @block.sync
            def _(sync, i=i, j=j, vv=vv):
                sync.wait_ge(ctx.sem["v"], vv)
                ctx.dma_on(sync.dma_start(out=ap[dst32][i * 128:(i + 1) * 128, :], in_=sq[j][:, :]), ssem_[j])
                ctx.dma_on(sync.dma_start(out=ap[dstb][i * 128:(i + 1) * 128, :], in_=ob[j][:, :]), ssem_[j])
            store_d[j] = ctx.n[ssem_[j]]
        ctx.barrier(block)



def phase_mamba_proj(ctx, block, ap, d):
    with contextlib.ExitStack() as st:
        xnb_sb = [sbuf(ctx, st, f"mp{d}_xnb{k}", [128, TOK], BF16) for k in range(8)]
        wb = [sbuf(ctx, st, f"mp{d}_wb{j}", [128, 8, 128], BF16) for j in range(2)]
        xh = sbuf(ctx, st, f"mp{d}_xh", [128, TOK], F32)
        acc = sbuf(ctx, st, f"mp{d}_acc", [128, TOK], F32)
        xc32 = [sbuf(ctx, st, f"mp{d}_xc32{j}", [128, TOK], F32) for j in range(2)]
        xcb = [sbuf(ctx, st, f"mp{d}_xcb{j}", [128, TOK], BF16) for j in range(2)]
        szt = [sbuf(ctx, st, f"mp{d}_sz{j}", [128, TOK], F32) for j in range(2)]
        cvw = sbuf(ctx, st, f"mp{d}_cvw", [128, 16, DC], F32)
        cvb = sbuf(ctx, st, f"mp{d}_cvb", [128, 16], F32)
        ps = psum(ctx, st, f"mp{d}_ps", [128, TOK])

        @block.sync
        def _(sync):
            ctx.dma(sync.dma_start(out=cvw[:, :, :], in_=ap[f"convw_{d}"]))
            ctx.dma(sync.dma_start(out=cvb[:, :], in_=ap[f"convb_{d}"]))
            for k in range(8):
                ctx.dma(sync.dma_start(out=xnb_sb[k][:, :], in_=ap["xnb"][k * 128:(k + 1) * 128, :]))
        d_pre = ctx.n["d"]
        wsem = [f"mp{d}wA", f"mp{d}wB"]
        msem = [f"mp{d}sA", f"mp{d}sB"]
        ctx.mk(*wsem, *msem)

        t_base = ctx.n["t"]
        psum_free = 0
        psum_free_eng = "v"
        store_done = {0: 0, 1: 0}

        for c in range(32):
            j = c % 2

            @block.gpsimd
            def _(g, c=c):
                if c >= 2:
                    g.wait_ge(ctx.sem["t"], t_base + (c - 1))
                src = ap[f"ipT_{d}"][:, c * 128:(c + 1) * 128].rearrange("(k p) m -> p k m", p=128)
                ctx.dma_on(g.dma_start(out=wb[c % 2][:, :, :], in_=src), wsem[c % 2])
            g_w = ctx.n[wsem[c % 2]]

            @block.tensor
            def _(t, c=c, j=j, g_w=g_w, pf=psum_free, pfe=psum_free_eng):
                if c == 0:
                    t.wait_ge(ctx.sem["d"], d_pre)
                t.wait_ge(ctx.sem[wsem[j]], g_w)
                if pf:
                    t.wait_ge(ctx.sem[pfe], pf)
                for ch in range(NCH):
                    for k in range(8):
                        last = t.matmul(ps[:, ch * 512:(ch + 1) * 512], wb[c % 2][:, k, :],
                                        xnb_sb[k][:, ch * 512:(ch + 1) * 512],
                                        start=(k == 0), stop=(k == 7))
                ctx.inc(last, "t")
            tv = ctx.n["t"]

            if c < 16:
                @block.vector
                def _(v, c=c, j=j, tv=tv, sd=store_done[j]):
                    v.wait_ge(ctx.sem["t"], tv)
                    if sd:
                        v.wait_ge(ctx.sem[msem[j]], sd)
                    ctx.inc(v.tensor_copy(out=xh[:, :], in_=ps[:, :]), "v")
                psum_free = ctx.n["v"]
                psum_free_eng = "v"

                @block.vector
                def _(v, c=c, j=j):
                    v.tensor_scalar(out=acc[:, :], in0=xh[:, :], scalar1=cvw[:, c, 3:4],
                                    scalar2=cvb[:, c:c + 1], op0=ALU.mult, op1=ALU.add)
                    for b in range(B_LOC):
                        o = b * T
                        for L in (1, 2, 3):
                            if d == "f":
                                last = v.scalar_tensor_tensor(
                                    out=acc[:, o + L:o + T], in0=xh[:, o:o + T - L],
                                    scalar=cvw[:, c, 3 - L:4 - L], in1=acc[:, o + L:o + T],
                                    op0=ALU.mult, op1=ALU.add)
                            else:
                                last = v.scalar_tensor_tensor(
                                    out=acc[:, o:o + T - L], in0=xh[:, o + L:o + T],
                                    scalar=cvw[:, c, 3 - L:4 - L], in1=acc[:, o:o + T - L],
                                    op0=ALU.mult, op1=ALU.add)
                    ctx.inc(last, "v")
                v_conv = ctx.n["v"]

                @block.scalar
                def _(s, j=j, v_conv=v_conv, sd=store_done[j]):
                    s.wait_ge(ctx.sem["v"], v_conv)
                    if sd:
                        s.wait_ge(ctx.sem[msem[j]], sd)
                    ctx.inc(s.activation(out=xc32[j][:, :], in_=acc[:, :], func=AF.Silu), "s")
                s_silu = ctx.n["s"]

                @block.vector
                def _(v, j=j, s_silu=s_silu):
                    v.wait_ge(ctx.sem["s"], s_silu)
                    ctx.inc(v.tensor_copy(out=xcb[j][:, :], in_=xc32[j][:, :]), "v")
                v_cb = ctx.n["v"]

                @block.sync
                def _(sync, c=c, j=j, v_cb=v_cb):
                    sync.wait_ge(ctx.sem["v"], v_cb)
                    ctx.dma_on(sync.dma_start(out=ap[f"xc32_{d}"][c * 128:(c + 1) * 128, :], in_=xc32[j][:, :]), msem[j])
                    ctx.dma_on(sync.dma_start(out=ap[f"xcb_{d}"][c * 128:(c + 1) * 128, :], in_=xcb[j][:, :]), msem[j])
                store_done[j] = ctx.n[msem[j]]
            else:
                @block.scalar
                def _(s, j=j, tv=tv, sd=store_done[j]):
                    s.wait_ge(ctx.sem["t"], tv)
                    if sd:
                        s.wait_ge(ctx.sem[msem[j]], sd)
                    ctx.inc(s.activation(out=szt[j][:, :], in_=ps[:, :], func=AF.Silu), "s")
                psum_free = ctx.n["s"]
                psum_free_eng = "s"

                @block.sync
                def _(sync, c=c, j=j, s_z=psum_free):
                    sync.wait_ge(ctx.sem["s"], s_z)
                    ctx.dma_on(sync.dma_start(out=ap[f"sz32_{d}"][(c - 16) * 128:(c - 15) * 128, :],
                                              in_=szt[j][:, :]), msem[j])
                store_done[j] = ctx.n[msem[j]]
        ctx.barrier(block)

    with contextlib.ExitStack() as st:
        xpw = sbuf(ctx, st, f"xp{d}_w", [128, 16, 96], BF16)
        rhs = [sbuf(ctx, st, f"xp{d}_r{k}", [128, 512], BF16) for k in range(16)]
        dbc = sbuf(ctx, st, f"xp{d}_dbc", [96, TOK], F32)
        psx = psum(ctx, st, f"xp{d}_ps", [96, 1024])

        @block.gpsimd
        def _(g):
            ctx.gdma(g.dma_start(out=xpw[:, :, :],
                                 in_=ap[f"xpT_{d}"].rearrange("(k p) m -> p k m", p=128)))
        g_w = ctx.n["g"]
        xpr = f"xp{d}r"
        ctx.mk(xpr)

        t_base = ctx.n["t"]
        v_base = ctx.n["v"]
        xpr_t = {}

        @block.sync
        def _(sync):
            for ch in range(NCH):
                if ch >= 1:
                    sync.wait_ge(ctx.sem["t"], t_base + ch)
                for k in range(16):
                    ctx.dma_on(sync.dma_start(
                        out=rhs[k][:, :],
                        in_=ap[f"xcb_{d}"][k * 128:(k + 1) * 128, ch * 512:(ch + 1) * 512]), xpr)
                xpr_t[ch] = ctx.n[xpr]

        @block.tensor
        def _(t):
            t.wait_ge(ctx.sem["g"], g_w)
            for ch in range(NCH):
                t.wait_ge(ctx.sem[xpr], xpr_t[ch])
                if ch >= 2:
                    t.wait_ge(ctx.sem["v"], v_base + ch - 1)
                pb = (ch % 2) * 512
                for k in range(16):
                    last = t.matmul(psx[:, pb:pb + 512], xpw[:, k, :], rhs[k][:, :],
                                    start=(k == 0), stop=(k == 15))
                ctx.inc(last, "t")

        @block.vector
        def _(v):
            for ch in range(NCH):
                v.wait_ge(ctx.sem["t"], t_base + ch + 1)
                pb = (ch % 2) * 512
                ctx.inc(v.tensor_copy(out=dbc[:, ch * 512:(ch + 1) * 512], in_=psx[:, pb:pb + 512]), "v")
        v_dbc = ctx.n["v"]

        @block.sync
        def _(sync):
            sync.wait_ge(ctx.sem["v"], v_dbc)
            ctx.dma(sync.dma_start(out=ap[f"dbc_{d}"][:, :], in_=dbc[:, :]))
        ctx.barrier(block)

    with contextlib.ExitStack() as st:
        dtw = sbuf(ctx, st, f"dl{d}_w", [64, DI], BF16)
        dtb16 = sbuf(ctx, st, f"dl{d}_dt", [64, TOK], BF16)
        dbc_sb = sbuf(ctx, st, f"dl{d}_dbc", [64, TOK], F32)
        dtbc = sbuf(ctx, st, f"dl{d}_dtb", [128, 16], F32)
        t0 = [sbuf(ctx, st, f"dl{d}_t0{j}", [128, TOK], F32) for j in range(2)]
        abt = [sbuf(ctx, st, f"dl{d}_ab{j}", [128, TOK], F32) for j in range(2)]
        r0 = [sbuf(ctx, st, f"dl{d}_r0{j}", [128, TOK], F32) for j in range(2)]
        psd = psum(ctx, st, f"dl{d}_ps", [128, TOK])

        @block.gpsimd
        def _(g):
            ctx.gdma(g.dma_start(out=dtw[:, :], in_=ap[f"dtwT_{d}"]))
        g_w = ctx.n["g"]

        @block.sync
        def _(sync):
            ctx.dma(sync.dma_start(out=dbc_sb[:, :], in_=ap[f"dbc_{d}"][0:64, :]))
            ctx.dma(sync.dma_start(out=dtbc[:, :], in_=ap[f"dtb_{d}"]))
        d_l = ctx.n["d"]

        @block.vector
        def _(v):
            v.wait_ge(ctx.sem["d"], d_l)
            ctx.inc(v.tensor_copy(out=dtb16[:, :], in_=dbc_sb[:, :]), "v")
        v16 = ctx.n["v"]

        psum_free = 0
        t0_free = {0: 0, 1: 0}
        st_done = {0: 0, 1: 0}
        dsm = [f"dl{d}sA", f"dl{d}sB"]
        ctx.mk(*dsm)
        for i in range(16):
            j = i % 2

            @block.tensor
            def _(t, i=i, pf=psum_free):
                if i == 0:
                    t.wait_ge(ctx.sem["g"], g_w)
                    t.wait_ge(ctx.sem["v"], v16)
                if pf:
                    t.wait_ge(ctx.sem["v"], pf)
                for ch in range(NCH):
                    last = t.matmul(psd[:, ch * 512:(ch + 1) * 512], dtw[:, i * 128:(i + 1) * 128],
                                    dtb16[:, ch * 512:(ch + 1) * 512], start=True, stop=True)
                ctx.inc(last, "t")
            t_i = ctx.n["t"]

            @block.vector
            def _(v, i=i, j=j, t_i=t_i, tf=t0_free[j]):
                v.wait_ge(ctx.sem["t"], t_i)
                if tf:
                    v.wait_ge(ctx.sem["v"], tf)
                ctx.inc(v.tensor_scalar(out=t0[j][:, :], in0=psd[:, :], scalar1=dtbc[:, i:i + 1],
                                        scalar2=None, op0=ALU.add), "v")
            psum_free = ctx.n["v"]

            @block.scalar
            def _(s, j=j, vp=psum_free):
                s.wait_ge(ctx.sem["v"], vp)
                s.activation(out=abt[j][:, :], in_=t0[j][:, :], func=AF.Abs)
                s.activation(out=abt[j][:, :], in_=abt[j][:, :], func=AF.Exp, scale=-1.0)
                ctx.inc(s.activation(out=abt[j][:, :], in_=abt[j][:, :], func=AF.Ln, bias=1.0), "s")
            s_sp = ctx.n["s"]

            @block.vector
            def _(v, j=j, s_sp=s_sp, sd=st_done[j]):
                v.wait_ge(ctx.sem["s"], s_sp)
                if sd:
                    v.wait_ge(ctx.sem[dsm[j]], sd)
                v.tensor_scalar(out=r0[j][:, :], in0=t0[j][:, :], scalar1=0.0, scalar2=None, op0=ALU.max)
                ctx.inc(v.tensor_tensor(out=r0[j][:, :], in0=r0[j][:, :], in1=abt[j][:, :], op=ALU.add), "v")
            t0_free[j] = ctx.n["v"]

            @block.sync
            def _(sync, i=i, j=j, vr=t0_free[j]):
                sync.wait_ge(ctx.sem["v"], vr)
                ctx.dma_on(sync.dma_start(out=ap[f"delta_{d}"][i * 128:(i + 1) * 128, :], in_=r0[j][:, :]), dsm[j])
            st_done[j] = ctx.n[dsm[j]]
        ctx.barrier(block)



def phase_mamba_scan(ctx, block, ap, d):
    TC = 256
    NTC = T // TC
    rev = (d == "b")
    with contextlib.ExitStack() as st:
        negA = sbuf(ctx, st, f"sc{d}_negA", [128, 16, DS], F32)
        Dcol = sbuf(ctx, st, f"sc{d}_D", [128, 16], F32)
        Bbc = sbuf(ctx, st, f"sc{d}_B", [128, DS, TC], F32)
        Cbc = sbuf(ctx, st, f"sc{d}_C", [128, DS, TC], F32)
        abuf = [sbuf(ctx, st, f"sc{d}_a{j}", [128, DS * TC], F32) for j in range(2)]
        dub = [sbuf(ctx, st, f"sc{d}_du{j}", [128, DS * TC], F32) for j in range(2)]
        hbuf = sbuf(ctx, st, f"sc{d}_h", [128, DS * TC], F32)
        pbuf = sbuf(ctx, st, f"sc{d}_p", [128, DS * TC], F32)
        dti = [sbuf(ctx, st, f"sc{d}_dt{j}", [128, TC], F32) for j in range(2)]
        xti = [sbuf(ctx, st, f"sc{d}_xc{j}", [128, TC], F32) for j in range(2)]
        zti = [sbuf(ctx, st, f"sc{d}_sz{j}", [128, TC], F32) for j in range(2)]
        dut = [sbuf(ctx, st, f"sc{d}_dut{j}", [128, TC], F32) for j in range(2)]
        ytmp = sbuf(ctx, st, f"sc{d}_yt", [128, TC], F32)
        yb16 = [sbuf(ctx, st, f"sc{d}_yb{j}", [128, TC], BF16) for j in range(2)]
        carry = sbuf(ctx, st, f"sc{d}_carry", [128, 16 * DS], F32)

        @block.sync
        def _(sync):
            ctx.dma(sync.dma_start(out=negA[:, :, :], in_=ap[f"Alog_{d}"]))
            ctx.dma(sync.dma_start(out=Dcol[:, :], in_=ap[f"Dsk_{d}"]))
        d_cst = ctx.n["d"]

        @block.scalar
        def _(s):
            s.wait_ge(ctx.sem["d"], d_cst)
            ctx.inc(s.activation(out=negA[:, :, :], in_=negA[:, :, :], func=AF.Exp), "s")
        s_A = ctx.n["s"]

        @block.vector
        def _(v):
            v.wait_ge(ctx.sem["s"], s_A)
            ctx.inc(v.tensor_scalar(out=negA[:, :, :], in0=negA[:, :, :], scalar1=-1.0,
                                    scalar2=None, op0=ALU.mult), "v")
        v_A = ctx.n["v"]

        vscan_done = {0: 0, 1: 0}
        tile_free = {0: 0, 1: 0}
        store_done = {0: 0, 1: 0}
        it = 0
        scl = [f"sc{d}lA", f"sc{d}lB"]
        scs = [f"sc{d}sA", f"sc{d}sB"]
        scbc = f"sc{d}bc"
        ctx.mk(*scl, *scs, scbc)

        for b in range(B_LOC):
            @block.vector
            def _(v, b=b):
                if b == 0:
                    v.wait_ge(ctx.sem["v"], v_A)
                ctx.inc(v.memset(carry[:, :], 0.0), "v")

            tcs = list(range(NTC))
            if rev:
                tcs = tcs[::-1]
            for tc in tcs:
                col0 = b * T + tc * TC
                @block.sync
                def _(sync, col0=col0):
                    ctx.wait(sync, "v")
                    ctx.wait(sync, "p")
                    for h2 in range(2):
                        ctx.dma_on(sync.dma_start(out=Bbc[:, h2 * 8:(h2 + 1) * 8, :],
                                                  in_=bcast_rows(ap[f"dbc_{d}"][64 + h2 * 8:64 + (h2 + 1) * 8,
                                                                                col0:col0 + TC])), scbc)
                        ctx.dma_on(sync.dma_start(out=Cbc[:, h2 * 8:(h2 + 1) * 8, :],
                                                  in_=bcast_rows(ap[f"dbc_{d}"][80 + h2 * 8:80 + (h2 + 1) * 8,
                                                                                col0:col0 + TC])), scbc)
                d_bc = ctx.n[scbc]

                for i in range(16):
                    j = it % 2

                    @block.sync
                    def _(sync, i=i, j=j, col0=col0, tf=tile_free[j], sd=store_done[j]):
                        if tf:
                            sync.wait_ge(ctx.sem["v"], tf)
                        if sd:
                            sync.wait_ge(ctx.sem[scs[j]], sd)
                        r0_ = i * 128
                        ctx.dma_on(sync.dma_start(out=dti[j][:, :], in_=ap[f"delta_{d}"][r0_:r0_ + 128, col0:col0 + TC]), scl[j])
                        ctx.dma_on(sync.dma_start(out=xti[j][:, :], in_=ap[f"xc32_{d}"][r0_:r0_ + 128, col0:col0 + TC]), scl[j])
                        ctx.dma_on(sync.dma_start(out=zti[j][:, :], in_=ap[f"sz32_{d}"][r0_:r0_ + 128, col0:col0 + TC]), scl[j])
                    d_ld = ctx.n[scl[j]]

                    @block.scalar
                    def _(s, i=i, j=j, d_ld=d_ld, vf=vscan_done[j]):
                        s.wait_ge(ctx.sem[scl[j]], d_ld)
                        if vf:
                            s.wait_ge(ctx.sem["v"], vf)
                        for si in range(DS):
                            last = s.activation(out=abuf[j][:, si * TC:(si + 1) * TC],
                                                in_=dti[j][:, :], func=AF.Exp,
                                                scale=negA[:, i, si:si + 1])
                        ctx.inc(last, "s")
                    s_a = ctx.n["s"]

                    @block.gpsimd
                    def _(g, j=j, d_ld=d_ld, d_bc=d_bc, vf=vscan_done[j]):
                        g.wait_ge(ctx.sem[scl[j]], d_ld)
                        g.wait_ge(ctx.sem[scbc], d_bc)
                        if vf:
                            g.wait_ge(ctx.sem["v"], vf)
                        g.tensor_tensor(out=dut[j][:, :], in0=dti[j][:, :], in1=xti[j][:, :],
                                        op=ALU.mult)
                        for si in range(DS):
                            last = g.tensor_tensor(out=dub[j][:, si * TC:(si + 1) * TC],
                                                   in0=dut[j][:, :], in1=Bbc[:, si, :], op=ALU.mult)
                        ctx.inc(last, "p")
                    g_dub = ctx.n["p"]

                    @block.vector
                    def _(v, i=i, j=j, s_a=s_a, g_dub=g_dub):
                        v.wait_ge(ctx.sem["s"], s_a)
                        v.wait_ge(ctx.sem["p"], g_dub)
                        for si in range(DS):
                            lo, hi = si * TC, (si + 1) * TC
                            if rev:
                                a_ap = abuf[j][:, hi - 1:lo - 1:-1] if lo > 0 else abuf[j][:, hi - 1::-1]
                                b_ap = dub[j][:, hi - 1:lo - 1:-1] if lo > 0 else dub[j][:, hi - 1::-1]
                            else:
                                a_ap = abuf[j][:, lo:hi]
                                b_ap = dub[j][:, lo:hi]
                            v.tensor_tensor_scan(out=hbuf[:, lo:hi], data0=a_ap, data1=b_ap,
                                                 initial=carry[:, i * DS + si:i * DS + si + 1],
                                                 op0=ALU.mult, op1=ALU.add)
                        v.tensor_copy(out=carry[:, i * DS:(i + 1) * DS], in_=hbuf[:, TC - 1::TC])
                        for si in range(DS):
                            lo, hi = si * TC, (si + 1) * TC
                            c_ap = Cbc[:, si, TC - 1::-1] if rev else Cbc[:, si, :]
                            v.tensor_tensor(out=pbuf[:, lo:hi], in0=hbuf[:, lo:hi], in1=c_ap, op=ALU.mult)
                        ctx.inc(v.tensor_tensor(out=pbuf[:, 0:8 * TC], in0=pbuf[:, 0:8 * TC],
                                                in1=pbuf[:, 8 * TC:16 * TC], op=ALU.add), "v")
                    vscan_done[j] = ctx.n["v"]

                    @block.vector
                    def _(v, i=i, j=j):
                        v.tensor_tensor(out=pbuf[:, 0:4 * TC], in0=pbuf[:, 0:4 * TC],
                                        in1=pbuf[:, 4 * TC:8 * TC], op=ALU.add)
                        v.tensor_tensor(out=pbuf[:, 0:2 * TC], in0=pbuf[:, 0:2 * TC],
                                        in1=pbuf[:, 2 * TC:4 * TC], op=ALU.add)
                        v.tensor_tensor(out=pbuf[:, 0:TC], in0=pbuf[:, 0:TC],
                                        in1=pbuf[:, TC:2 * TC], op=ALU.add)
                        ysrc = pbuf[:, TC - 1::-1] if rev else pbuf[:, 0:TC]
                        v.scalar_tensor_tensor(out=ytmp[:, :], in0=xti[j][:, :],
                                               scalar=Dcol[:, i:i + 1], in1=ysrc,
                                               op0=ALU.mult, op1=ALU.add)
                        ctx.inc(v.tensor_tensor(out=yb16[j][:, :], in0=ytmp[:, :], in1=zti[j][:, :],
                                                op=ALU.mult), "v")
                    tile_free[j] = ctx.n["v"]

                    @block.sync
                    def _(sync, i=i, j=j, col0=col0, vy=tile_free[j]):
                        sync.wait_ge(ctx.sem["v"], vy)
                        ctx.dma_on(sync.dma_start(out=ap[f"yb_{d}"][i * 128:(i + 1) * 128, col0:col0 + TC],
                                                  in_=yb16[j][:, :]), scs[j])
                    store_done[j] = ctx.n[scs[j]]
                    it += 1
        ctx.barrier(block)



def phase_outproj_combine(ctx, block, ap):
    with contextlib.ExitStack() as st:
        opw = {dd: sbuf(ctx, st, f"op_{dd}", [128, 16, D], BF16) for dd in ("f", "b")}
        yrt = {dd: [sbuf(ctx, st, f"opy_{dd}{k}", [128, 512], BF16) for k in range(16)] for dd in ("f", "b")}
        xnt = [sbuf(ctx, st, f"op_xn{j}", [128, 512], F32) for j in range(2)]
        hout = [sbuf(ctx, st, f"op_ho{j}", [128, 512], F32) for j in range(2)]
        pso = psum(ctx, st, "op_ps", [128, 1024])

        @block.gpsimd
        def _(g):
            for dd in ("f", "b"):
                ctx.gdma(g.dma_start(out=opw[dd][:, :, :],
                                     in_=ap[f"opT_{dd}"].rearrange("(k p) m -> p k m", p=128)))
        g_w = ctx.n["g"]

        store_done = {0: 0, 1: 0}
        ctx.mk("oyr", "oxA", "oxB", "ohA", "ohB")
        oxs = ["oxA", "oxB"]
        ohs = ["ohA", "ohB"]
        for ch in range(NCH):
            @block.sync
            def _(sync, ch=ch):
                ctx.wait(sync, "t")
                for dd in ("f", "b"):
                    for k in range(16):
                        ctx.dma_on(sync.dma_start(
                            out=yrt[dd][k][:, :],
                            in_=ap[f"yb_{dd}"][k * 128:(k + 1) * 128, ch * 512:(ch + 1) * 512]), "oyr")
            d_y = ctx.n["oyr"]

            for o in range(8):
                j = o % 2

                @block.tensor
                def _(t, ch=ch, o=o, j=j, d_y=d_y, pf=psum_free[j]):
                    if o == 0 and ch == 0:
                        t.wait_ge(ctx.sem["g"], g_w)
                    t.wait_ge(ctx.sem["oyr"], d_y)
                    if pf:
                        t.wait_ge(ctx.sem["v"], pf)
                    pb = j * 512
                    first = True
                    for dd in ("f", "b"):
                        for k in range(16):
                            last = t.matmul(pso[:, pb:pb + 512], opw[dd][:, k, o * 128:(o + 1) * 128],
                                            yrt[dd][k][:, :], start=first,
                                            stop=(dd == "b" and k == 15))
                            first = False
                    ctx.inc(last, "t")
                t_o = ctx.n["t"]

                @block.sync
                def _(sync, ch=ch, o=o, j=j, xf=psum_free[j]):
                    if xf:
                        sync.wait_ge(ctx.sem["v"], xf)
                    ctx.dma_on(sync.dma_start(out=xnt[j][:, :],
                                              in_=ap["xn32"][o * 128:(o + 1) * 128,
                                                             ch * 512:(ch + 1) * 512]), oxs[j])
                d_xn = ctx.n[oxs[j]]

                @block.vector
                def _(v, ch=ch, o=o, j=j, t_o=t_o, d_xn=d_xn, sd=store_done[j]):
                    v.wait_ge(ctx.sem["t"], t_o)
                    v.wait_ge(ctx.sem[oxs[j]], d_xn)
                    if sd:
                        v.wait_ge(ctx.sem[ohs[j]], sd)
                    pb = j * 512
                    ctx.inc(v.tensor_tensor(out=hout[j][:, :], in0=pso[:, pb:pb + 512],
                                            in1=xnt[j][:, :], op=ALU.add), "v")
                psum_free[j] = ctx.n["v"]

                @block.sync
                def _(sync, ch=ch, o=o, j=j, vh=psum_free[j]):
                    sync.wait_ge(ctx.sem["v"], vh)
                    ctx.dma_on(sync.dma_start(out=ap["h32"][o * 128:(o + 1) * 128,
                                                            ch * 512:(ch + 1) * 512],
                                              in_=hout[j][:, :]), ohs[j])
                store_done[j] = ctx.n[ohs[j]]
        ctx.barrier(block)
    phase_ln(ctx, block, ap, "h32", "attn32", "attnb", wcol="lnw", bcol="lnb", tag="m")


psum_free = {0: 0, 1: 0}



def phase_l0_ffn(ctx, block, ap):
    with contextlib.ExitStack() as st:
        gw = sbuf(ctx, st, "ff_gw", [128, 8, E], F32)
        xnb_sb = [sbuf(ctx, st, f"ff_xnb{k}", [128, TOK], F32) for k in range(8)]
        logit = sbuf(ctx, st, "ff_logit", [E, TOK], F32)
        idf = sbuf(ctx, st, "ff_idf", [128, 128], F32)
        ptile = sbuf(ctx, st, "ff_pt", [128, E], F32)
        prob = sbuf(ctx, st, "ff_prob", [128, E], F32)
        mrow = sbuf(ctx, st, "ff_m", [128, 1], F32)
        srow = sbuf(ctx, st, "ff_s", [128, 1], F32)
        cnt = sbuf(ctx, st, "ff_cnt", [128, 1], F32)
        cmp_ = sbuf(ctx, st, "ff_cmp", [128, E], F32)
        gtile = sbuf(ctx, st, "ff_gt", [128, E], F32)
        psg = psum(ctx, st, "ff_psg", [E, 1024])
        pst = psum(ctx, st, "ff_pst", [128, E])
        pst2 = psum(ctx, st, "ff_pst2", [E, 128])

        @block.sync
        def _(sync):
            ctx.dma(sync.dma_start(out=gw[:, :, :],
                                   in_=ap["gwT"].rearrange("(k p) m -> p k m", p=128)))
            ctx.dma(sync.dma_start(out=idf[:, :], in_=ap["identf"]))
            for k in range(8):
                ctx.dma(sync.dma_start(out=xnb_sb[k][:, :], in_=ap["xn32"][k * 128:(k + 1) * 128, :]))
        d_x = ctx.n["d"]

        t_base = ctx.n["t"]
        v_base = ctx.n["v"]

        @block.tensor
        def _(t):
            t.wait_ge(ctx.sem["d"], d_x)
            for ch in range(NCH):
                if ch >= 2:
                    t.wait_ge(ctx.sem["v"], v_base + ch - 1)
                pb = (ch % 2) * 512
                for k in range(8):
                    last = t.matmul(psg[:, pb:pb + 512], gw[:, k, :],
                                    xnb_sb[k][:, ch * 512:(ch + 1) * 512],
                                    start=(k == 0), stop=(k == 7))
                ctx.inc(last, "t")

        @block.vector
        def _(v):
            for ch in range(NCH):
                v.wait_ge(ctx.sem["t"], t_base + ch + 1)
                pb = (ch % 2) * 512
                ctx.inc(v.tensor_copy(out=logit[:, ch * 512:(ch + 1) * 512], in_=psg[:, pb:pb + 512]), "v")
        v_log = ctx.n["v"]

        @block.sync
        def _(sync):
            sync.wait_ge(ctx.sem["v"], v_log)
            ctx.dma(sync.dma_start(out=ap["logits_dbg"][:, :], in_=logit[:, :]))
        d_logdbg = ctx.n["d"]

        @block.tensor
        def _(t):
            t.wait_ge(ctx.sem["d"], d_logdbg)

        for blk in range(TOK // 128):
            @block.tensor
            def _(t, blk=blk, first=(blk == 0)):
                if first:
                    t.wait_ge(ctx.sem["v"], v_log)
                else:
                    t.wait_ge(ctx.sem["v"], gate_done)
                ctx.inc(t.transpose(pst[:, :], logit[0:E, blk * 128:(blk + 1) * 128], idf[0:E, 0:E]), "t")
            t_tp = ctx.n["t"]

            @block.vector
            def _(v, blk=blk, t_tp=t_tp):
                v.wait_ge(ctx.sem["t"], t_tp)
                ctx.vedge(v, v.tensor_copy(out=ptile[:, :], in_=pst[:, :]))
                ctx.inc(v.reduce_max(out=mrow[:, :], in_=ptile[:, :],
                                     axis=mybir.AxisListType.X, negate=True), "v")
            v_m = ctx.n["v"]

            @block.scalar
            def _(s, v_m=v_m):
                s.wait_ge(ctx.sem["v"], v_m)
                ctx.inc(s.activation(out=prob[:, :], in_=ptile[:, :], func=AF.Exp,
                                     bias=mrow[:, 0:1], accum_out=srow[:, 0:1]), "s")
            s_e = ctx.n["s"]

            @block.vector
            def _(v, blk=blk, s_e=s_e):
                v.wait_ge(ctx.sem["s"], s_e)
                ctx.vedge(v, v.reciprocal(out=srow[:, :], in_=srow[:, :]))
                ctx.vedge(v, v.tensor_scalar(out=prob[:, :], in0=prob[:, :], scalar1=srow[:, 0:1],
                                             scalar2=None, op0=ALU.mult))
                for e in range(E):
                    ctx.vedge(v, v.tensor_tensor(out=cmp_[:, :], in0=prob[:, :],
                                                 in1=prob[:, e:e + 1].broadcast_to([128, E]), op=ALU.is_gt))
                    ctx.vedge(v, v.reduce_sum(out=cnt[:, :], in_=cmp_[:, :], axis=mybir.AxisListType.X))
                    if e > 0:
                        ctx.vedge(v, v.tensor_tensor(out=cmp_[:, 0:e], in0=prob[:, 0:e],
                                                     in1=prob[:, e:e + 1].broadcast_to([128, e]), op=ALU.is_equal))
                        ctx.vedge(v, v.reduce_sum(out=cmp_[:, 0:1], in_=cmp_[:, 0:e], axis=mybir.AxisListType.X))
                        ctx.vedge(v, v.tensor_tensor(out=cnt[:, :], in0=cnt[:, :], in1=cmp_[:, 0:1], op=ALU.add))
                    ctx.vedge(v, v.tensor_scalar(out=cnt[:, :], in0=cnt[:, :], scalar1=2.0, scalar2=None, op0=ALU.is_lt))
                    ctx.vedge(v, v.tensor_tensor(out=gtile[:, e:e + 1], in0=prob[:, e:e + 1], in1=cnt[:, :], op=ALU.mult))
            v_g = ctx.n["v"]

            @block.tensor
            def _(t, blk=blk, v_g=v_g):
                t.wait_ge(ctx.sem["v"], v_g)
                ctx.inc(t.transpose(pst2[:, :], gtile[:, 0:E], idf[:, :]), "t")
            t_g = ctx.n["t"]

            @block.vector
            def _(v, blk=blk, t_g=t_g):
                v.wait_ge(ctx.sem["t"], t_g)
                ctx.inc(v.tensor_copy(out=logit[0:E, blk * 128:(blk + 1) * 128], in_=pst2[0:E, :]), "v")
            gate_done = ctx.n["v"]

        @block.sync
        def _(sync):
            ctx.wait(sync, "v")
            ctx.dma(sync.dma_start(out=ap["gates"][:, :], in_=logit[:, :]))
        ctx.barrier(block)

    for e in range(E + 1):
        shared = (e == E)
        with contextlib.ExitStack() as st:
            w1h = sbuf(ctx, st, f"fe{e}_w1", [128, 8, FF // 2], BF16)
            w3h = sbuf(ctx, st, f"fe{e}_w3", [128, 8, FF // 2], BF16)
            w2h = sbuf(ctx, st, f"fe{e}_w2", [128, 16, D], BF16)
            xch = [sbuf(ctx, st, f"fe{e}_x{k}", [128, 512], BF16) for k in range(8)]
            gt32 = sbuf(ctx, st, f"fe{e}_g", [128, 512], F32)
            hb = [sbuf(ctx, st, f"fe{e}_h{k2}", [128, 512], BF16) for k2 in range(16)]
            gbc = sbuf(ctx, st, f"fe{e}_gbc", [128, 512], F32)
            ystage = [sbuf(ctx, st, f"fe{e}_y{j}", [128, 8, 512], F32) for j in range(1)]
            psg_ = psum(ctx, st, f"fe{e}_psg", [128, 1024])
            psu_ = psum(ctx, st, f"fe{e}_psu", [128, 1024])
            psy2 = psum(ctx, st, f"fe{e}_psy2", [128, 1024])

            ctx.mk("fx", "fw", "fy")
            for fh in range(2):
                @block.gpsimd
                def _(g, e=e, fh=fh, shared=shared):
                    ctx.wait(g, "t")
                    if shared:
                        w1 = ap["w1s"][:, fh * 2048:(fh + 1) * 2048]
                        w3 = ap["w3s"][:, fh * 2048:(fh + 1) * 2048]
                        w2 = ap["w2s"][fh * 2048:(fh + 1) * 2048, :]
                    else:
                        w1 = ap["w1e"][e, :, fh * 2048:(fh + 1) * 2048]
                        w3 = ap["w3e"][e, :, fh * 2048:(fh + 1) * 2048]
                        w2 = ap["w2e"][e, fh * 2048:(fh + 1) * 2048, :]
                    ctx.dma_on(g.dma_start(out=w1h[:, :, :], in_=w1.rearrange("(k p) m -> p k m", p=128)), "fw")
                    ctx.dma_on(g.dma_start(out=w3h[:, :, :], in_=w3.rearrange("(k p) m -> p k m", p=128)), "fw")
                    ctx.dma_on(g.dma_start(out=w2h[:, :, :], in_=w2.rearrange("(k p) m -> p k m", p=128)), "fw")
                g_w = ctx.n["fw"]

                for ch in range(NCH):
                    @block.sync
                    def _(sync, e=e, ch=ch, shared=shared):
                        ctx.wait(sync, "t")
                        ctx.wait(sync, "v")
                        for k in range(8):
                            ctx.dma_on(sync.dma_start(out=xch[k][:, :],
                                       in_=ap["xnb"][k * 128:(k + 1) * 128, ch * 512:(ch + 1) * 512]), "fx")
                        if not shared:
                            ctx.dma_on(sync.dma_start(out=gbc[:, :],
                                                      in_=bcast_rows(ap["gates"][e:e + 1, ch * 512:(ch + 1) * 512])
                                                      .rearrange("p o n -> p (o n)")), "fx")
                    d_x = ctx.n["fx"]

                    if shared:
                        @block.vector
                        def _(v, d_x=d_x):
                            ctx.wait(v, "v")
                            ctx.inc(v.memset(gbc[:, :], 0.125), "v")
                        v_gb = ctx.n["v"]
                    else:
                        v_gb = 0

                    for ft in range(16):
                        @block.tensor
                        def _(t, ft=ft, d_x=d_x, g_w=g_w):
                            t.wait_ge(ctx.sem["fw"], g_w)
                            t.wait_ge(ctx.sem["fx"], d_x)
                            if hfree[ft % 2]:
                                t.wait_ge(ctx.sem["v"], hfree[ft % 2])
                            pb = (ft % 2) * 512
                            for k in range(8):
                                t.matmul(psg_[:, pb:pb + 512], w1h[:, k, ft * 128:(ft + 1) * 128],
                                         xch[k][:, :], start=(k == 0), stop=(k == 7))
                            for k in range(8):
                                last = t.matmul(psu_[:, pb:pb + 512], w3h[:, k, ft * 128:(ft + 1) * 128],
                                                xch[k][:, :], start=(k == 0), stop=(k == 7))
                            ctx.inc(last, "t")
                        t_gu = ctx.n["t"]

                        @block.scalar
                        def _(s, ft=ft, t_gu=t_gu):
                            s.wait_ge(ctx.sem["t"], t_gu)
                            pb = (ft % 2) * 512
                            ctx.inc(s.activation(out=gt32[:, :], in_=psg_[:, pb:pb + 512],
                                                 func=AF.Silu), "s")
                        s_g = ctx.n["s"]

                        @block.vector
                        def _(v, ft=ft, s_g=s_g):
                            v.wait_ge(ctx.sem["s"], s_g)
                            pb = (ft % 2) * 512
                            ctx.inc(v.tensor_tensor(out=hb[ft][:, :], in0=gt32[:, :],
                                                    in1=psu_[:, pb:pb + 512], op=ALU.mult), "v")
                        hfree[ft % 2] = ctx.n["v"]
                    h_all = ctx.n["v"]

                    for dt_ in range(8):
                        @block.tensor
                        def _(t, dt_=dt_, h_all=h_all):
                            t.wait_ge(ctx.sem["v"], h_all)
                            if yfree[dt_ % 2]:
                                t.wait_ge(ctx.sem["v"], yfree[dt_ % 2])
                            pb = (dt_ % 2) * 512
                            for ft in range(16):
                                last = t.matmul(psy2[:, pb:pb + 512], w2h[:, ft, dt_ * 128:(dt_ + 1) * 128],
                                                hb[ft][:, :], start=(ft == 0), stop=(ft == 15))
                            ctx.inc(last, "t")
                        t_y = ctx.n["t"]

                        @block.vector
                        def _(v, dt_=dt_, t_y=t_y, v_gb=v_gb, d_x=d_x):
                            v.wait_ge(ctx.sem["t"], t_y)
                            if v_gb:
                                v.wait_ge(ctx.sem["v"], v_gb)
                            else:
                                v.wait_ge(ctx.sem["fx"], d_x)
                            if stage_free[0]:
                                v.wait_ge(ctx.sem["fy"], stage_free[0])
                            pb = (dt_ % 2) * 512
                            ctx.inc(v.scalar_tensor_tensor(out=ystage[0][:, dt_, :], in0=psy2[:, pb:pb + 512],
                                                           scalar=0.5, in1=gbc[:, :],
                                                           op0=ALU.mult, op1=ALU.mult), "v")
                        yfree[dt_ % 2] = ctx.n["v"]
                    v_ys = ctx.n["v"]

                    @block.gpsimd
                    def _(g, e=e, fh=fh, ch=ch, v_ys=v_ys):
                        g.wait_ge(ctx.sem["v"], v_ys)
                        first = (e == 0 and fh == 0)
                        ctx.dma_on(g.dma_start(
                            out=ap["ffn32"].rearrange("(o p) n -> p o n", p=128)[:, :, ch * 512:(ch + 1) * 512],
                            in_=ystage[0][:, :, :],
                            accum_op=(ALU.bypass if first else ALU.add)), "fy")
                    stage_free[0] = ctx.n["fy"]
        ctx.barrier(block)


hfree = {0: 0, 1: 0}
yfree = {0: 0, 1: 0}
stage_free = {0: 0}



def phase_residual3(ctx, block, ap, a, b_, c, dst32, dstb):
    with contextlib.ExitStack() as st:
        ta = sbuf(ctx, st, "r3_a", [128, TOK], F32)
        tb = sbuf(ctx, st, "r3_b", [128, TOK], F32)
        tcm = sbuf(ctx, st, "r3_c", [128, TOK], F32)
        ob = sbuf(ctx, st, "r3_ob", [128, TOK], BF16)
        for i in range(8):
            @block.sync
            def _(sync, i=i):
                ctx.wait(sync, "d")
                ctx.wait(sync, "v")
                ctx.dma(sync.dma_start(out=ta[:, :], in_=ap[a][i * 128:(i + 1) * 128, :]))
                ctx.dma(sync.dma_start(out=tb[:, :], in_=ap[b_][i * 128:(i + 1) * 128, :]))
                ctx.dma(sync.dma_start(out=tcm[:, :], in_=ap[c][i * 128:(i + 1) * 128, :]))
            d_l = ctx.n["d"]

            @block.vector
            def _(v, d_l=d_l):
                v.wait_ge(ctx.sem["d"], d_l)
                v.tensor_tensor(out=ta[:, :], in0=ta[:, :], in1=tb[:, :], op=ALU.add)
                v.tensor_tensor(out=ta[:, :], in0=ta[:, :], in1=tcm[:, :], op=ALU.add)
                ctx.inc(v.tensor_copy(out=ob[:, :], in_=ta[:, :]), "v")
            v_r = ctx.n["v"]

            @block.sync
            def _(sync, i=i, v_r=v_r):
                sync.wait_ge(ctx.sem["v"], v_r)
                ctx.dma(sync.dma_start(out=ap[dst32][i * 128:(i + 1) * 128, :], in_=ta[:, :]))
                ctx.dma(sync.dma_start(out=ap[dstb][i * 128:(i + 1) * 128, :], in_=ob[:, :]))
        ctx.barrier(block)




def phase_router_probs(ctx, block, ap):
    with contextlib.ExitStack() as st:
        gw = sbuf(ctx, st, "rt_gw", [128, 8, E], F32)
        bsc = sbuf(ctx, st, "rt_bs", [E, 1], F32)
        xns = [sbuf(ctx, st, f"rt_x{k}", [128, TOK], F32) for k in range(8)]
        logit = sbuf(ctx, st, "rt_logit", [E, TOK], F32)
        idf = sbuf(ctx, st, "rt_idf", [128, 128], F32)
        ptile = sbuf(ctx, st, "rt_pt", [128, E], F32)
        prob = sbuf(ctx, st, "rt_prob", [128, E], F32)
        mrow = sbuf(ctx, st, "rt_m", [128, 1], F32)
        srow = sbuf(ctx, st, "rt_s", [128, 1], F32)
        psg = psum(ctx, st, "rt_psg", [E, 1024])
        pst = psum(ctx, st, "rt_pst", [128, E])
        pst2 = psum(ctx, st, "rt_pst2", [E, 128])

        @block.sync
        def _(sync):
            ctx.dma(sync.dma_start(out=gw[:, :, :], in_=ap["ws"].rearrange("(k p) m -> p k m", p=128)))
            ctx.dma(sync.dma_start(out=bsc[:, :], in_=ap["bs"]))
            ctx.dma(sync.dma_start(out=idf[:, :], in_=ap["identf"]))
            for k in range(8):
                ctx.dma(sync.dma_start(out=xns[k][:, :], in_=ap["xn1_32"][k * 128:(k + 1) * 128, :]))
        d_x = ctx.n["d"]

        t_base = ctx.n["t"]
        v_base = ctx.n["v"]

        @block.tensor
        def _(t):
            t.wait_ge(ctx.sem["d"], d_x)
            for ch in range(NCH):
                if ch >= 2:
                    t.wait_ge(ctx.sem["v"], v_base + ch - 1)
                pb = (ch % 2) * 512
                for k in range(8):
                    last = t.matmul(psg[:, pb:pb + 512], gw[:, k, :],
                                    xns[k][:, ch * 512:(ch + 1) * 512],
                                    start=(k == 0), stop=(k == 7))
                ctx.inc(last, "t")

        @block.vector
        def _(v):
            for ch in range(NCH):
                v.wait_ge(ctx.sem["t"], t_base + ch + 1)
                pb = (ch % 2) * 512
                ctx.inc(v.tensor_scalar(out=logit[:, ch * 512:(ch + 1) * 512], in0=psg[:, pb:pb + 512],
                                        scalar1=bsc[0:E, 0:1], scalar2=None, op0=ALU.add), "v")
        v_log = ctx.n["v"]

        gate_done = 0
        for blk in range(TOK // 128):
            @block.tensor
            def _(t, blk=blk, first=(blk == 0), gd=gate_done):
                if first:
                    t.wait_ge(ctx.sem["v"], v_log)
                else:
                    t.wait_ge(ctx.sem["v"], gd)
                ctx.inc(t.transpose(pst[:, :], logit[0:E, blk * 128:(blk + 1) * 128], idf[0:E, 0:E]), "t")
            t_tp = ctx.n["t"]

            @block.vector
            def _(v, t_tp=t_tp):
                v.wait_ge(ctx.sem["t"], t_tp)
                ctx.vedge(v, v.tensor_copy(out=ptile[:, :], in_=pst[:, :]))
                ctx.inc(v.reduce_max(out=mrow[:, :], in_=ptile[:, :],
                                     axis=mybir.AxisListType.X, negate=True), "v")
            v_m = ctx.n["v"]

            @block.scalar
            def _(s, v_m=v_m):
                s.wait_ge(ctx.sem["v"], v_m)
                ctx.inc(s.activation(out=prob[:, :], in_=ptile[:, :], func=AF.Exp,
                                     bias=mrow[:, 0:1], accum_out=srow[:, 0:1]), "s")
            s_e = ctx.n["s"]

            @block.vector
            def _(v, s_e=s_e):
                v.wait_ge(ctx.sem["s"], s_e)
                ctx.vedge(v, v.reciprocal(out=srow[:, :], in_=srow[:, :]))
                ctx.inc(v.tensor_scalar(out=prob[:, :], in0=prob[:, :], scalar1=srow[:, 0:1],
                                        scalar2=None, op0=ALU.mult), "v")
            v_p = ctx.n["v"]

            @block.tensor
            def _(t, v_p=v_p):
                t.wait_ge(ctx.sem["v"], v_p)
                ctx.inc(t.transpose(pst2[:, :], prob[:, 0:E], idf[:, :]), "t")
            t_g = ctx.n["t"]

            @block.vector
            def _(v, blk=blk, t_g=t_g):
                v.wait_ge(ctx.sem["t"], t_g)
                ctx.inc(v.tensor_copy(out=logit[0:E, blk * 128:(blk + 1) * 128], in_=pst2[0:E, :]), "v")
            gate_done = ctx.n["v"]

        @block.sync
        def _(sync):
            ctx.wait(sync, "v")
            ctx.dma(sync.dma_start(out=ap["probs1"][:, :], in_=logit[:, :]))
        ctx.barrier(block)


def phase_l1_select(ctx, block, ap):
    NT = NCORES * TOK
    FW = NT // 128
    KSEL = float(NT // E)
    import os as _os
    NIT = int(_os.environ.get("SEL_NIT", "42"))
    with contextlib.ExitStack() as st:
        pt = [sbuf(ctx, st, f"sl_p{e}", [128, FW], F32) for e in range(E)]
        ones256 = sbuf(ctx, st, "sl_ones", [128, FW], F32)
        onescol = sbuf(ctx, st, "sl_onec", [128, 1], F32)
        onesrow = sbuf(ctx, st, "sl_oner", [1, 128], F32)
        idf = sbuf(ctx, st, "sl_idf", [128, 128], F32)
        lo = sbuf(ctx, st, "sl_lo", [128, E], F32)
        hi = sbuf(ctx, st, "sl_hi", [128, E], F32)
        mid = sbuf(ctx, st, "sl_mid", [128, E], F32)
        msk = sbuf(ctx, st, "sl_msk", [128, FW], F32)
        cnt8 = sbuf(ctx, st, "sl_cnt", [128, E], F32)
        tot8 = sbuf(ctx, st, "sl_tot", [1, E], F32)
        cmpu = sbuf(ctx, st, "sl_cmpu", [128, E], mybir.dt.uint8)
        cmpu2 = sbuf(ctx, st, "sl_cmpu2", [128, E], mybir.dt.uint8)
        mstrict = sbuf(ctx, st, "sl_mst", [128, FW], F32)
        meq = sbuf(ctx, st, "sl_meq", [128, FW], F32)
        pref = sbuf(ctx, st, "sl_pref", [128, FW], F32)
        needc = sbuf(ctx, st, "sl_need", [128, 1], F32)
        rrow = sbuf(ctx, st, "sl_rrow", [1, 128], F32)
        erow = sbuf(ctx, st, "sl_erow", [1, 128], F32)
        ocol = sbuf(ctx, st, "sl_ocol", [128, 1], F32)
        wloc = sbuf(ctx, st, "sl_w", [128, FW], F32)
        ps_tot = psum(ctx, st, "sl_pstot", [1, E])
        ps_bc = psum(ctx, st, "sl_psbc", [128, E])
        ps_t1 = psum(ctx, st, "sl_pst1", [1, 128])
        ps_t2 = psum(ctx, st, "sl_pst2", [128, 1])

        @block.sync
        def _(sync):
            for e in range(E):
                ctx.dma(sync.dma_start(out=pt[e][:, :],
                                       in_=ap["probsF"][e:e + 1, :].rearrange("o (p f) -> (o p) f", p=128)))
            ctx.dma(sync.dma_start(out=onescol[:, :], in_=ap["ones128x1"]))
            ctx.dma(sync.dma_start(out=onesrow[:, :], in_=ap["ones1x128"]))
            ctx.dma(sync.dma_start(out=idf[:, :], in_=ap["identf"]))
        d_p = ctx.n["d"]

        @block.vector
        def _(v):
            v.wait_ge(ctx.sem["d"], d_p)
            v.memset(ones256[:, :], 1.0)
            v.memset(lo[:, :], 0.0)
            ctx.inc(v.memset(hi[:, :], 1.0), "v")
        v_init = ctx.n["v"]

        @block.tensor
        def _(t):
            t.wait_ge(ctx.sem["v"], v_init)

        for it in range(NIT):
            @block.vector
            def _(v, it=it):
                if it > 0:
                    v.wait_ge(ctx.sem["t"], bc_t)
                ctx.vedge(v, v.tensor_tensor(out=mid[:, :], in0=lo[:, :], in1=hi[:, :], op=ALU.add))
                ctx.vedge(v, v.tensor_scalar(out=mid[:, :], in0=mid[:, :], scalar1=0.5, scalar2=None, op0=ALU.mult))
                for e in range(E):
                    ctx.vedge(v, v.tensor_scalar(out=msk[:, :], in0=pt[e][:, :], scalar1=mid[:, e:e + 1],
                                                 scalar2=None, op0=ALU.is_gt))
                    ctx.vedge(v, v.reduce_sum(out=cnt8[:, e:e + 1], in_=msk[:, :], axis=mybir.AxisListType.X))
            v_cnt = ctx.n["v"]

            @block.tensor
            def _(t, v_cnt=v_cnt):
                t.wait_ge(ctx.sem["v"], v_cnt)
                for e in range(E):
                    last = t.matmul(ps_tot[0:1, e:e + 1], onescol[:, 0:1], cnt8[:, e:e + 1],
                                    start=True, stop=True)
                ctx.inc(last, "t")
            t_tot = ctx.n["t"]

            @block.vector
            def _(v, t_tot=t_tot):
                v.wait_ge(ctx.sem["t"], t_tot)
                ctx.inc(v.tensor_copy(out=tot8[:, :], in_=ps_tot[0:1, :]), "v")
            v_tot = ctx.n["v"]

            @block.tensor
            def _(t, v_tot=v_tot):
                t.wait_ge(ctx.sem["v"], v_tot)
                ctx.inc(t.matmul(ps_bc[:, :], onesrow[0:1, :], tot8[0:1, :], start=True, stop=True), "t")
            bc_t = ctx.n["t"]

            @block.vector
            def _(v, bc_t=bc_t):
                v.wait_ge(ctx.sem["t"], bc_t)
                ctx.vedge(v, v.tensor_scalar(out=cmpu[:, :], in0=ps_bc[:, :], scalar1=KSEL,
                                             scalar2=None, op0=ALU.is_ge))
                ctx.vedge(v, v.tensor_scalar(out=cmpu2[:, :], in0=ps_bc[:, :], scalar1=KSEL,
                                             scalar2=None, op0=ALU.is_lt))
                ctx.vedge(v, v.copy_predicated(out=lo[:, :], mask=cmpu[:, :], data=mid[:, :]))
                ctx.vedge(v, v.copy_predicated(out=hi[:, :], mask=cmpu2[:, :], data=mid[:, :]))
            bc_t = ctx.n["t"]

        for e in range(E):
            @block.vector
            def _(v, e=e):
                ctx.vedge(v, v.tensor_scalar(out=mstrict[:, :], in0=pt[e][:, :], scalar1=hi[:, e:e + 1],
                                             scalar2=None, op0=ALU.is_gt))
                ctx.vedge(v, v.reduce_sum(out=cnt8[:, 0:1], in_=mstrict[:, :], axis=mybir.AxisListType.X))
            v_sc = ctx.n["v"]

            @block.tensor
            def _(t, v_sc=v_sc):
                t.wait_ge(ctx.sem["v"], v_sc)
                ctx.inc(t.matmul(ps_tot[0:1, 0:1], onescol[:, 0:1], cnt8[:, 0:1], start=True, stop=True), "t")
            t_sc = ctx.n["t"]

            @block.vector
            def _(v, t_sc=t_sc):
                v.wait_ge(ctx.sem["t"], t_sc)
                ctx.inc(v.tensor_copy(out=tot8[:, 0:1], in_=ps_tot[0:1, 0:1]), "v")
            v_t2 = ctx.n["v"]

            @block.tensor
            def _(t, v_t2=v_t2):
                t.wait_ge(ctx.sem["v"], v_t2)
                ctx.inc(t.matmul(ps_bc[:, 0:1], onesrow[0:1, :], tot8[0:1, 0:1], start=True, stop=True), "t")
            t_bc2 = ctx.n["t"]

            @block.vector
            def _(v, e=e, t_bc2=t_bc2):
                v.wait_ge(ctx.sem["t"], t_bc2)
                ctx.vedge(v, v.tensor_scalar(out=needc[:, :], in0=ps_bc[:, 0:1], scalar1=-1.0,
                                             scalar2=KSEL, op0=ALU.mult, op1=ALU.add))
                ctx.vedge(v, v.tensor_scalar(out=meq[:, :], in0=pt[e][:, :], scalar1=lo[:, e:e + 1],
                                             scalar2=None, op0=ALU.is_gt))
                ctx.vedge(v, v.tensor_tensor(out=meq[:, :], in0=meq[:, :], in1=mstrict[:, :], op=ALU.subtract))
                ctx.vedge(v, v.tensor_tensor_scan(out=pref[:, :], data0=ones256[:, :], data1=meq[:, :],
                                                  initial=0.0, op0=ALU.mult, op1=ALU.add))
            v_pref = ctx.n["v"]

            @block.tensor
            def _(t, v_pref=v_pref):
                t.wait_ge(ctx.sem["v"], v_pref)
                ctx.inc(t.transpose(ps_t1[0:1, :], pref[:, FW - 1:FW], idf[:, :]), "t")
            t_tr = ctx.n["t"]

            @block.vector
            def _(v, t_tr=t_tr):
                v.wait_ge(ctx.sem["t"], t_tr)
                ctx.vedge(v, v.tensor_copy(out=rrow[:, :], in_=ps_t1[0:1, :]))
                ctx.vedge(v, v.tensor_tensor_scan(out=rrow[:, :], data0=onesrow[0:1, :], data1=rrow[0:1, :],
                                                  initial=0.0, op0=ALU.mult, op1=ALU.add))
                ctx.vedge(v, v.memset(erow[0:1, 0:1], 0.0))
                ctx.vedge(v, v.tensor_copy(out=erow[0:1, 1:128], in_=rrow[0:1, 0:127]))
            v_er = ctx.n["v"]

            @block.tensor
            def _(t, v_er=v_er):
                t.wait_ge(ctx.sem["v"], v_er)
                ctx.inc(t.transpose(ps_t2[:, 0:1], erow[0:1, :], idf[0:1, 0:1]), "t")
            t_tr2 = ctx.n["t"]

            @block.vector
            def _(v, e=e, t_tr2=t_tr2):
                v.wait_ge(ctx.sem["t"], t_tr2)
                ctx.vedge(v, v.tensor_copy(out=ocol[:, :], in_=ps_t2[:, 0:1]))
                ctx.vedge(v, v.tensor_tensor(out=pref[:, :], in0=pref[:, :], in1=meq[:, :], op=ALU.subtract))
                ctx.vedge(v, v.tensor_scalar(out=pref[:, :], in0=pref[:, :], scalar1=ocol[:, 0:1],
                                             scalar2=None, op0=ALU.add))
                ctx.vedge(v, v.tensor_scalar(out=pref[:, :], in0=pref[:, :], scalar1=needc[:, 0:1],
                                             scalar2=None, op0=ALU.is_lt))
                ctx.vedge(v, v.tensor_tensor(out=meq[:, :], in0=meq[:, :], in1=pref[:, :], op=ALU.mult))
                ctx.vedge(v, v.tensor_tensor(out=mstrict[:, :], in0=mstrict[:, :], in1=meq[:, :], op=ALU.add))
                ctx.inc(v.tensor_tensor(out=wloc[:, :], in0=pt[e][:, :], in1=mstrict[:, :], op=ALU.mult), "v")
            v_w = ctx.n["v"]

            @block.sync
            def _(sync, e=e, v_w=v_w):
                sync.wait_ge(ctx.sem["v"], v_w)
                ctx.dma(sync.dma_start(
                    out=ap["gates1"][e:e + 1, :].rearrange("o (p f) -> (o p) f", p=16),
                    in_=wloc[0:16, :]))
        ctx.barrier(block)


def phase_l1_attn(ctx, block, ap):
    with contextlib.ExitStack() as st:
        xnT = [sbuf(ctx, st, f"qk_x{k}", [128, TOK], BF16) for k in range(8)]
        wq = sbuf(ctx, st, "qk_w", [128, 8, 512], BF16)
        bbc = sbuf(ctx, st, "qk_b", [128, 3 * D], F32)
        cost = sbuf(ctx, st, "qk_cos", [128, 16, 32], F32)
        sint = sbuf(ctx, st, "qk_sin", [128, 16, 32], F32)
        ev = [sbuf(ctx, st, f"qk_e{j}", [128, 512], F32) for j in range(2)]
        re_ = sbuf(ctx, st, "qk_re", [128, 256], F32)
        ro_ = sbuf(ctx, st, "qk_ro", [128, 256], F32)
        tmp_ = sbuf(ctx, st, "qk_tmp", [128, 256], F32)
        ob = [sbuf(ctx, st, f"qk_o{j}", [128, 512], BF16) for j in range(2)]
        psq = psum(ctx, st, "qk_ps", [128, 1024])
        ctx.mk("qkw", "qks0", "qks1")
        qks = ["qks0", "qks1"]

        @block.sync
        def _(sync):
            for k in range(8):
                ctx.dma(sync.dma_start(out=xnT[k][:, :], in_=ap["xn1b"][k * 128:(k + 1) * 128, :]))
            ctx.dma(sync.dma_start(out=bbc[:, :],
                                   in_=bcast_rows(ap["bqkv"][0:1, :]).rearrange("p o n -> p (o n)")))
            ctx.dma(sync.dma_start(out=cost[:, :, :],
                                   in_=ap["cosT"].rearrange("(i p) s -> p i s", p=128)))
            ctx.dma(sync.dma_start(out=sint[:, :, :],
                                   in_=ap["sinT"].rearrange("(i p) s -> p i s", p=128)))
        d_c = ctx.n["d"]

        it = 0
        pfree = {0: 0, 1: 0}
        sdone = {0: 0, 1: 0}
        for ncb in range(6):
            @block.gpsimd
            def _(g, ncb=ncb):
                ctx.wait(g, "t")
                ctx.dma_on(g.dma_start(out=wq[:, :, :],
                                       in_=ap["wqkv"][:, ncb * 512:(ncb + 1) * 512]
                                       .rearrange("(k p) m -> p k m", p=128)), "qkw")
            g_w = ctx.n["qkw"]

            for tc in range(TOK // 128):
                j = it % 2

                @block.tensor
                def _(t, ncb=ncb, tc=tc, j=j, g_w=g_w, pf=pfree[j], first=(it == 0)):
                    if first:
                        t.wait_ge(ctx.sem["d"], d_c)
                    t.wait_ge(ctx.sem["qkw"], g_w)
                    if pf:
                        t.wait_ge(ctx.sem["v"], pf)
                    pb = j * 512
                    for k in range(8):
                        last = t.matmul(psq[:, pb:pb + 512], xnT[k][:, tc * 128:(tc + 1) * 128],
                                        wq[:, k, :], start=(k == 0), stop=(k == 7))
                    ctx.inc(last, "t")
                t_q = ctx.n["t"]

                @block.vector
                def _(v, ncb=ncb, tc=tc, j=j, t_q=t_q, sd=sdone[j]):
                    v.wait_ge(ctx.sem["t"], t_q)
                    if sd:
                        v.wait_ge(ctx.sem[qks[j]], sd)
                    pb = j * 512
                    ctx.inc(v.tensor_tensor(out=ev[j][:, :], in0=psq[:, pb:pb + 512],
                                            in1=bbc[:, ncb * 512:(ncb + 1) * 512], op=ALU.add), "v")
                pfree[j] = ctx.n["v"]

                @block.vector
                def _(v, ncb=ncb, tc=tc, j=j):
                    if ncb < 4:
                        e3 = ev[j][:, :].rearrange("p (h s two) -> p h s two", h=8, two=2)
                        evn = e3[:, :, :, 0]
                        odd = e3[:, :, :, 1]
                        cosb = cost[:, tc % 16, :].unsqueeze(1).broadcast_to([128, 8, 32])
                        sinb = sint[:, tc % 16, :].unsqueeze(1).broadcast_to([128, 8, 32])
                        r3 = re_[:, :].rearrange("p (h s) -> p h s", h=8)
                        o3 = ro_[:, :].rearrange("p (h s) -> p h s", h=8)
                        t3 = tmp_[:, :].rearrange("p (h s) -> p h s", h=8)
                        v.tensor_tensor(out=r3, in0=evn, in1=cosb, op=ALU.mult)
                        v.tensor_tensor(out=t3, in0=odd, in1=sinb, op=ALU.mult)
                        v.tensor_tensor(out=r3, in0=r3, in1=t3, op=ALU.subtract)
                        v.tensor_tensor(out=o3, in0=evn, in1=sinb, op=ALU.mult)
                        v.tensor_tensor(out=t3, in0=odd, in1=cosb, op=ALU.mult)
                        v.tensor_tensor(out=o3, in0=o3, in1=t3, op=ALU.add)
                        ob3 = ob[j][:, :].rearrange("p (h s two) -> p h s two", h=8, two=2)
                        v.tensor_copy(out=ob3[:, :, :, 0], in_=r3)
                        ctx.inc(v.tensor_copy(out=ob3[:, :, :, 1], in_=o3), "v")
                    else:
                        ctx.inc(v.tensor_copy(out=ob[j][:, :], in_=ev[j][:, :]), "v")
                v_o = ctx.n["v"]

                @block.sync
                def _(sync, ncb=ncb, tc=tc, j=j, v_o=v_o):
                    sync.wait_ge(ctx.sem["v"], v_o)
                    ctx.dma_on(sync.dma_start(
                        out=ap["qkvb"][tc * 128:(tc + 1) * 128, ncb * 512:(ncb + 1) * 512],
                        in_=ob[j][:, :]), qks[j])
                sdone[j] = ctx.n[qks[j]]
                it += 1
        ctx.barrier(block)

    NQT = T // 128
    with contextlib.ExitStack() as st:
        identb = sbuf(ctx, st, "at_idb", [128, 128], BF16)
        caus = sbuf(ctx, st, "at_caus", [128, 128], F32)
        qtm = [sbuf(ctx, st, f"at_qtm{k}", [128, 64], BF16) for k in range(NQT)]
        ktm = [sbuf(ctx, st, f"at_ktm{k}", [128, 64], BF16) for k in range(NQT)]
        vtm = [sbuf(ctx, st, f"at_vtm{k}", [128, 64], BF16) for k in range(NQT)]
        qT = sbuf(ctx, st, "at_qT", [64, T], BF16)
        kT = sbuf(ctx, st, "at_kT", [64, T], BF16)
        strip = sbuf(ctx, st, "at_strip", [128, T], F32)
        pstr = sbuf(ctx, st, "at_pstr", [128, T], BF16)
        mrow = sbuf(ctx, st, "at_m", [128, 1], F32)
        srow = sbuf(ctx, st, "at_s", [128, 1], F32)
        ptb = [sbuf(ctx, st, f"at_ptb{k}", [128, 128], BF16) for k in range(NQT)]
        oT = sbuf(ctx, st, "at_oT", [64, T], F32)
        ps_s = psum(ctx, st, "at_pss", [128, 1024])
        ps_tpA = psum(ctx, st, "at_tpA", [128, 128], BF16)
        ps_tpB = psum(ctx, st, "at_tpB", [128, 128], BF16)
        ps_o = psum(ctx, st, "at_pso", [64, 1024])
        ctx.mk("atl", "ats")
        tpp = [ps_tpA, ps_tpB]

        @block.sync
        def _(sync):
            ctx.dma(sync.dma_start(out=identb[:, :], in_=ap["identb"]))
            ctx.dma(sync.dma_start(out=caus[:, :], in_=ap["causal"]))
        d_cst = ctx.n["d"]

        tp_done = {0: 0, 1: 0}
        ovac = {0: 0, 1: 0}
        for b in range(B_LOC):
            for h in range(NH):
                @block.sync
                def _(sync, b=b, h=h):
                    ctx.wait(sync, "t")
                    ctx.wait(sync, "v")
                    for k in range(NQT):
                        r0 = b * T + k * 128
                        ctx.dma_on(sync.dma_start(out=qtm[k][:, :],
                                   in_=ap["qkvb"][r0:r0 + 128, h * 64:(h + 1) * 64]), "atl")
                        ctx.dma_on(sync.dma_start(out=ktm[k][:, :],
                                   in_=ap["qkvb"][r0:r0 + 128, D + h * 64:D + (h + 1) * 64]), "atl")
                        ctx.dma_on(sync.dma_start(out=vtm[k][:, :],
                                   in_=ap["qkvb"][r0:r0 + 128, 2 * D + h * 64:2 * D + (h + 1) * 64]), "atl")
                d_qkv = ctx.n["atl"]

                for k in range(NQT):
                    for which in range(2):
                        sl = k * 2 + which
                        par = sl % 2

                        @block.tensor
                        def _(t, k=k, which=which, par=par, d_qkv=d_qkv, first=(k == 0 and which == 0),
                              tpf=tp_done[par]):
                            if first:
                                t.wait_ge(ctx.sem["d"], d_cst)
                                t.wait_ge(ctx.sem["atl"], d_qkv)
                            if tpf:
                                t.wait_ge(ctx.sem["v"], tpf)
                            src_t = qtm[k] if which == 0 else ktm[k]
                            ctx.inc(t.transpose(tpp[par][0:64, :], src_t[:, :], identb[:, :]), "t")
                        tv = ctx.n["t"]

                        @block.vector
                        def _(v, k=k, which=which, par=par, tv=tv):
                            v.wait_ge(ctx.sem["t"], tv)
                            dst = qT if which == 0 else kT
                            ctx.inc(v.tensor_copy(out=dst[:, k * 128:(k + 1) * 128],
                                                  in_=tpp[par][0:64, :]), "v")
                        tp_done[par] = ctx.n["v"]
                v_tp = ctx.n["v"]

                strip_done = 0
                for qt in range(NQT):
                    L = (qt + 1) * 128
                    evac_v = {0: 0, 1: 0}
                    for kb in range(qt + 1):
                        pb = (kb % 2) * 512

                        @block.tensor
                        def _(t, qt=qt, kb=kb, pb=pb, v_tp=v_tp, sd=strip_done, ef=evac_v[kb % 2]):
                            if kb == 0:
                                t.wait_ge(ctx.sem["v"], v_tp if qt == 0 else sd)
                            if ef:
                                t.wait_ge(ctx.sem["v"], ef)
                            ctx.inc(t.matmul(ps_s[:, pb:pb + 128], qT[:, qt * 128:(qt + 1) * 128],
                                             kT[:, kb * 128:(kb + 1) * 128], start=True, stop=True), "t")
                        tv = ctx.n["t"]

                        @block.vector
                        def _(v, qt=qt, kb=kb, pb=pb, tv=tv):
                            v.wait_ge(ctx.sem["t"], tv)
                            if kb == qt:
                                ctx.inc(v.tensor_tensor(out=strip[:, kb * 128:(kb + 1) * 128],
                                                        in0=ps_s[:, pb:pb + 128], in1=caus[:, :],
                                                        op=ALU.add), "v")
                            else:
                                ctx.inc(v.tensor_copy(out=strip[:, kb * 128:(kb + 1) * 128],
                                                      in_=ps_s[:, pb:pb + 128]), "v")
                        evac_v[kb % 2] = ctx.n["v"]

                    @block.vector
                    def _(v, qt=qt, L=L):
                        ctx.vedge(v, v.reduce_max(out=mrow[:, :], in_=strip[:, 0:L],
                                                  axis=mybir.AxisListType.X, negate=True))
                        ctx.inc(v.tensor_scalar(out=mrow[:, :], in0=mrow[:, :], scalar1=0.125,
                                                scalar2=None, op0=ALU.mult), "v")
                    v_m = ctx.n["v"]

                    @block.scalar
                    def _(s, qt=qt, L=L, v_m=v_m):
                        s.wait_ge(ctx.sem["v"], v_m)
                        ctx.inc(s.activation(out=pstr[:, 0:L], in_=strip[:, 0:L], func=AF.Exp,
                                             scale=0.125, bias=mrow[:, 0:1],
                                             accum_out=srow[:, 0:1]), "s")
                    s_p = ctx.n["s"]

                    @block.vector
                    def _(v, qt=qt, L=L, s_p=s_p):
                        v.wait_ge(ctx.sem["s"], s_p)
                        ctx.vedge(v, v.reciprocal(out=srow[:, :], in_=srow[:, :]))
                        ctx.inc(v.tensor_scalar(out=pstr[:, 0:L], in0=pstr[:, 0:L],
                                                scalar1=srow[:, 0:1], scalar2=None, op0=ALU.mult), "v")
                    v_pn = ctx.n["v"]

                    for kb in range(qt + 1):
                        par = kb % 2

                        @block.tensor
                        def _(t, qt=qt, kb=kb, par=par, v_pn=v_pn, tpf=tp_done[par]):
                            if kb == 0:
                                t.wait_ge(ctx.sem["v"], v_pn)
                            if tpf:
                                t.wait_ge(ctx.sem["v"], tpf)
                            ctx.inc(t.transpose(tpp[par][:, :], pstr[:, kb * 128:(kb + 1) * 128],
                                                identb[:, :]), "t")
                        tv = ctx.n["t"]

                        @block.vector
                        def _(v, kb=kb, par=par, tv=tv):
                            v.wait_ge(ctx.sem["t"], tv)
                            ctx.inc(v.tensor_copy(out=ptb[kb][:, :], in_=tpp[par][:, :]), "v")
                        tp_done[par] = ctx.n["v"]
                    v_ptb = ctx.n["v"]

                    @block.tensor
                    def _(t, qt=qt, v_ptb=v_ptb, of=ovac[qt % 2]):
                        t.wait_ge(ctx.sem["v"], v_ptb)
                        if of:
                            t.wait_ge(ctx.sem["v"], of)
                        pb3 = (qt % 2) * 512
                        for kb in range(qt + 1):
                            last = t.matmul(ps_o[:, pb3:pb3 + 128], vtm[kb][:, :],
                                            ptb[kb][:, :], start=(kb == 0), stop=(kb == qt))
                        ctx.inc(last, "t")
                    t_pv = ctx.n["t"]

                    @block.vector
                    def _(v, qt=qt, t_pv=t_pv):
                        v.wait_ge(ctx.sem["t"], t_pv)
                        pb3 = (qt % 2) * 512
                        ctx.inc(v.tensor_copy(out=oT[:, qt * 128:(qt + 1) * 128],
                                              in_=ps_o[0:64, pb3:pb3 + 128]), "v")
                    ovac[qt % 2] = ctx.n["v"]
                    strip_done = ctx.n["v"]

                v_all = ctx.n["v"]

                @block.sync
                def _(sync, b=b, h=h, v_all=v_all):
                    sync.wait_ge(ctx.sem["v"], v_all)
                    ctx.dma_on(sync.dma_start(
                        out=ap["attnT32"][h * 64:(h + 1) * 64, b * T:(b + 1) * T],
                        in_=oT[:, :]), "ats")
        ctx.barrier(block)

    with contextlib.ExitStack() as st:
        at32 = [sbuf(ctx, st, f"pj_a{j}", [128, TOK], F32) for j in range(2)]
        atb = [sbuf(ctx, st, f"pj_ab{j}", [128, TOK], BF16) for j in range(2)]
        ctx.mk("pjl", "pjs")

        for k in range(8):
            j = k % 2

            @block.sync
            def _(sync, k=k, j=j):
                if k >= 2:
                    ctx.wait(sync, "v")
                ctx.dma_on(sync.dma_start(out=at32[j][:, :], in_=ap["attnT32"][k * 128:(k + 1) * 128, :]), "pjl")
            d_l = ctx.n["pjl"]

            @block.vector
            def _(v, k=k, j=j, d_l=d_l, sp=0):
                v.wait_ge(ctx.sem["pjl"], d_l)
                ctx.wait(v, "pjs")
                ctx.inc(v.tensor_copy(out=atb[j][:, :], in_=at32[j][:, :]), "v")
            v_c = ctx.n["v"]

            @block.sync
            def _(sync, k=k, j=j, v_c=v_c):
                sync.wait_ge(ctx.sem["v"], v_c)
                ctx.dma_on(sync.dma_start(out=ap["attnTb"][k * 128:(k + 1) * 128, :], in_=atb[j][:, :]), "pjs")
        ctx.barrier(block)

    with contextlib.ExitStack() as st:
        wp = sbuf(ctx, st, "pj_w", [128, 8, 128], BF16)
        rhs = [sbuf(ctx, st, f"pj_r{k}", [128, 512], BF16) for k in range(8)]
        bpc = sbuf(ctx, st, "pj_bp", [128, 8], F32)
        outt = [sbuf(ctx, st, f"pj_o{j}", [128, 512], F32) for j in range(2)]
        psp = psum(ctx, st, "pj_ps", [128, 1024])
        ctx.mk("pjw", "pjr", "pjo0", "pjo1")
        pjo = ["pjo0", "pjo1"]

        @block.sync
        def _(sync):
            ctx.dma(sync.dma_start(out=bpc[:, :], in_=ap["bproj"]))
        d_b = ctx.n["d"]

        it = 0
        pfree2 = {0: 0, 1: 0}
        sdone2 = {0: 0, 1: 0}
        for o in range(8):
            @block.gpsimd
            def _(g, o=o):
                ctx.wait(g, "t")
                ctx.dma_on(g.dma_start(out=wp[:, :, :],
                                       in_=ap["wproj"][:, o * 128:(o + 1) * 128]
                                       .rearrange("(k p) m -> p k m", p=128)), "pjw")
            g_w = ctx.n["pjw"]

            for ch in range(NCH):
                j = it % 2

                @block.sync
                def _(sync, o=o, ch=ch, first=(it == 0)):
                    ctx.wait(sync, "t")
                    for k in range(8):
                        ctx.dma_on(sync.dma_start(out=rhs[k][:, :],
                                                  in_=ap["attnTb"][k * 128:(k + 1) * 128,
                                                                   ch * 512:(ch + 1) * 512]), "pjr")
                d_r = ctx.n["pjr"]

                @block.tensor
                def _(t, o=o, ch=ch, j=j, g_w=g_w, d_r=d_r, pf=pfree2[j], d_b=d_b):
                    t.wait_ge(ctx.sem["pjw"], g_w)
                    t.wait_ge(ctx.sem["pjr"], d_r)
                    t.wait_ge(ctx.sem["d"], d_b)
                    if pf:
                        t.wait_ge(ctx.sem["v"], pf)
                    pb = j * 512
                    for k in range(8):
                        last = t.matmul(psp[:, pb:pb + 512], wp[:, k, :], rhs[k][:, :],
                                        start=(k == 0), stop=(k == 7))
                    ctx.inc(last, "t")
                t_p = ctx.n["t"]

                @block.vector
                def _(v, o=o, j=j, t_p=t_p, sd=sdone2[j]):
                    v.wait_ge(ctx.sem["t"], t_p)
                    if sd:
                        v.wait_ge(ctx.sem[pjo[j]], sd)
                    pb = j * 512
                    ctx.inc(v.tensor_scalar(out=outt[j][:, :], in0=psp[:, pb:pb + 512],
                                            scalar1=bpc[:, o:o + 1], scalar2=None, op0=ALU.add), "v")
                pfree2[j] = ctx.n["v"]

                @block.sync
                def _(sync, o=o, ch=ch, j=j, vv=pfree2[j]):
                    sync.wait_ge(ctx.sem["v"], vv)
                    ctx.dma_on(sync.dma_start(out=ap["attn1_32"][o * 128:(o + 1) * 128,
                                                                 ch * 512:(ch + 1) * 512],
                                              in_=outt[j][:, :]), pjo[j])
                sdone2[j] = ctx.n[pjo[j]]
                it += 1
        ctx.barrier(block)


ovac_done = {0: 0, 1: 0}


def phase_l1_ffn(ctx, block, ap):
    for e in range(E):
        with contextlib.ExitStack() as st:
            w1h = sbuf(ctx, st, f"f1{e}_w1", [128, 8, FF // 2], BF16)
            w2h = sbuf(ctx, st, f"f1{e}_w2", [128, 16, D], BF16)
            b1c = sbuf(ctx, st, f"f1{e}_b1", [128, 32], F32)
            b2c = sbuf(ctx, st, f"f1{e}_b2", [128, 8], F32)
            xch = [sbuf(ctx, st, f"f1{e}_x{k}", [128, 512], BF16) for k in range(8)]
            hb = [sbuf(ctx, st, f"f1{e}_h{k2}", [128, 512], BF16) for k2 in range(16)]
            gbc = sbuf(ctx, st, f"f1{e}_gbc", [128, 512], F32)
            ystage = sbuf(ctx, st, f"f1{e}_y", [128, 8, 512], F32)
            psg_ = psum(ctx, st, f"f1{e}_psg", [128, 1024])
            psy2 = psum(ctx, st, f"f1{e}_psy2", [128, 1024])

            ctx.mk("fx", "fw", "fy")
            for fh in range(2):
                @block.gpsimd
                def _(g, e=e, fh=fh):
                    ctx.wait(g, "t")
                    w1 = ap["w1L1"][e, :, fh * 2048:(fh + 1) * 2048]
                    w2 = ap["w2L1"][e, fh * 2048:(fh + 1) * 2048, :]
                    ctx.dma_on(g.dma_start(out=w1h[:, :, :], in_=w1.rearrange("(k p) m -> p k m", p=128)), "fw")
                    ctx.dma_on(g.dma_start(out=w2h[:, :, :], in_=w2.rearrange("(k p) m -> p k m", p=128)), "fw")
                g_w = ctx.n["fw"]

                @block.sync
                def _(sync, e=e, fh=fh):
                    ctx.wait(sync, "s")
                    ctx.dma(sync.dma_start(out=b1c[:, :], in_=ap["b1L1"][e]))
                    ctx.dma(sync.dma_start(out=b2c[:, :], in_=ap["b2L1"][e]))
                d_bias = ctx.n["d"]

                for ch in range(NCH):
                    @block.sync
                    def _(sync, e=e, ch=ch):
                        ctx.wait(sync, "t")
                        ctx.wait(sync, "v")
                        for k in range(8):
                            ctx.dma_on(sync.dma_start(out=xch[k][:, :],
                                       in_=ap["xn1b"][k * 128:(k + 1) * 128, ch * 512:(ch + 1) * 512]), "fx")
                        ctx.dma_on(sync.dma_start(out=gbc[:, :],
                                                  in_=bcast_rows(ap["gates1"][e:e + 1, ch * 512:(ch + 1) * 512])
                                                  .rearrange("p o n -> p (o n)")), "fx")
                    d_x = ctx.n["fx"]

                    for ft in range(16):
                        @block.tensor
                        def _(t, ft=ft, fh=fh, d_x=d_x, g_w=g_w):
                            t.wait_ge(ctx.sem["fw"], g_w)
                            t.wait_ge(ctx.sem["fx"], d_x)
                            if hfree[ft % 2]:
                                t.wait_ge(ctx.sem["s"], hfree[ft % 2])
                            pb = (ft % 2) * 512
                            for k in range(8):
                                last = t.matmul(psg_[:, pb:pb + 512], w1h[:, k, ft * 128:(ft + 1) * 128],
                                                xch[k][:, :], start=(k == 0), stop=(k == 7))
                            ctx.inc(last, "t")
                        t_gu = ctx.n["t"]

                        @block.scalar
                        def _(s, ft=ft, fh=fh, t_gu=t_gu, d_bias=d_bias):
                            s.wait_ge(ctx.sem["t"], t_gu)
                            s.wait_ge(ctx.sem["d"], d_bias)
                            pb = (ft % 2) * 512
                            ctx.inc(s.activation(out=hb[ft][:, :], in_=psg_[:, pb:pb + 512],
                                                 func=AF.Gelu_apprx_tanh,
                                                 bias=b1c[:, fh * 16 + ft:fh * 16 + ft + 1]), "s")
                        hfree[ft % 2] = ctx.n["s"]
                    h_all = ctx.n["s"]

                    for dt_ in range(8):
                        @block.tensor
                        def _(t, dt_=dt_, h_all=h_all):
                            t.wait_ge(ctx.sem["s"], h_all)
                            if yfree[dt_ % 2]:
                                t.wait_ge(ctx.sem["v"], yfree[dt_ % 2])
                            pb = (dt_ % 2) * 512
                            for ft in range(16):
                                last = t.matmul(psy2[:, pb:pb + 512], w2h[:, ft, dt_ * 128:(dt_ + 1) * 128],
                                                hb[ft][:, :], start=(ft == 0), stop=(ft == 15))
                            ctx.inc(last, "t")
                        t_y = ctx.n["t"]

                        @block.vector
                        def _(v, dt_=dt_, t_y=t_y, d_x=d_x):
                            v.wait_ge(ctx.sem["t"], t_y)
                            v.wait_ge(ctx.sem["fx"], d_x)
                            if stage_free[0]:
                                v.wait_ge(ctx.sem["fy"], stage_free[0])
                            pb = (dt_ % 2) * 512
                            ctx.inc(v.scalar_tensor_tensor(out=ystage[:, dt_, :], in0=psy2[:, pb:pb + 512],
                                                           scalar=b2c[:, dt_:dt_ + 1], in1=gbc[:, :],
                                                           op0=ALU.add, op1=ALU.mult), "v")
                        yfree[dt_ % 2] = ctx.n["v"]
                    v_ys = ctx.n["v"]

                    @block.gpsimd
                    def _(g, e=e, fh=fh, ch=ch, v_ys=v_ys):
                        g.wait_ge(ctx.sem["v"], v_ys)
                        first = (e == 0 and fh == 0)
                        ctx.dma_on(g.dma_start(
                            out=ap["ffn1_32"].rearrange("(o p) n -> p o n", p=128)[:, :, ch * 512:(ch + 1) * 512],
                            in_=ystage[:, :, :],
                            accum_op=(ALU.bypass if first else ALU.add)), "fy")
                    stage_free[0] = ctx.n["fy"]
        ctx.barrier(block)



def build_L1(dbg=()):
    global hfree, yfree, stage_free, ovac_done
    hfree = {0: 0, 1: 0}
    yfree = {0: 0, 1: 0}
    stage_free = {0: 0}
    ovac_done = {0: 0, 1: 0}
    nc = bass.Bass("TRN2", target_bir_lowering=False)
    ap = {}

    def din(name, shape, dtype=F32):
        ap[name] = nc.dram_tensor(name, list(shape), dtype, kind="ExternalInput").ap()

    def dout(name, shape, dtype=F32):
        ap[name] = nc.dram_tensor(name, list(shape), dtype, kind="ExternalOutput").ap()

    def dint(name, shape, dtype=F32):
        kind = "ExternalOutput" if name in dbg else "Internal"
        ap[name] = nc.dram_tensor(name, list(shape), dtype, kind=kind).ap()

    din("x1", (D, TOK)); din("xn1_32", (D, TOK)); din("xn1b", (D, TOK), BF16)
    din("probsF", (E, NCORES * TOK))
    din("wqkv", (D, 3 * D)); din("bqkv", (1, 3 * D))
    din("wproj", (D, D)); din("bproj", (128, 8))
    din("w1L1", (E, D, FF)); din("b1L1", (E, 128, 32))
    din("w2L1", (E, FF, D)); din("b2L1", (E, 128, 8))
    din("cosT", (T, 32)); din("sinT", (T, 32))
    din("identf", (128, 128)); din("identb", (128, 128), BF16); din("causal", (128, 128))
    din("ones1x128", (1, 128)); din("ones128x1", (128, 1))

    dint("gates1", (E, TOK))
    dint("qkvb", (TOK, 3 * D), BF16)
    dint("attnT32", (D, TOK)); dint("attnTb", (D, TOK), BF16)
    dint("attn1_32", (D, TOK)); dint("ffn1_32", (D, TOK))
    dint("rowbuf", (2, TOK))
    dout("x2", (D, TOK)); dout("x2b", (D, TOK), BF16)

    st = contextlib.ExitStack()
    ctx = Ctx(nc, st)
    limit = PHASE_LIMIT if DEBUG_LAUNCH == 1 else 99
    phases = [
        lambda: phase_l1_select(ctx, block, ap),
        lambda: phase_l1_attn(ctx, block, ap),
        lambda: phase_l1_ffn(ctx, block, ap),
        lambda: phase_residual3(ctx, block, ap, "x1", "attn1_32", "ffn1_32", "x2", "x2b"),
    ]
    with st, nc.Block() as block:
        for i, ph in enumerate(phases):
            if i < limit:
                ph()
    return nc



def build_L0(dbg=()):
    global psum_free, hfree, yfree, stage_free
    psum_free = {0: 0, 1: 0}
    hfree = {0: 0, 1: 0}
    yfree = {0: 0, 1: 0}
    stage_free = {0: 0}
    nc = bass.Bass("TRN2", target_bir_lowering=False)
    ap = {}

    def din(name, shape, dtype=F32):
        ap[name] = nc.dram_tensor(name, list(shape), dtype, kind="ExternalInput").ap()

    def dout(name, shape, dtype=F32):
        ap[name] = nc.dram_tensor(name, list(shape), dtype, kind="ExternalOutput").ap()

    def dint(name, shape, dtype=F32):
        kind = "ExternalOutput" if name in dbg else "Internal"
        ap[name] = nc.dram_tensor(name, list(shape), dtype, kind=kind).ap()

    din("xT", (D, TOK))
    for d in ("f", "b"):
        din(f"ipT_{d}", (D, 2 * DI)); din(f"xpT_{d}", (DI, DR + 2 * DS))
        din(f"dtwT_{d}", (DR, DI)); din(f"opT_{d}", (DI, D))
        din(f"convw_{d}", (128, 16, DC)); din(f"convb_{d}", (128, 16))
        din(f"dtb_{d}", (128, 16)); din(f"Alog_{d}", (128, 16, DS)); din(f"Dsk_{d}", (128, 16))
    din("lnw", (128, 8)); din("lnb", (128, 8))
    din("gwT", (D, E))
    din("w1e", (E, D, FF)); din("w3e", (E, D, FF)); din("w2e", (E, FF, D))
    din("w1s", (D, FF)); din("w3s", (D, FF)); din("w2s", (FF, D))
    din("ones1x128", (1, 128)); din("ones128x1", (128, 1)); din("identf", (128, 128))

    dint("rowbuf", (2, TOK))
    dint("xn32", (D, TOK)); dint("xnb", (D, TOK), BF16)
    for d in ("f", "b"):
        dint(f"xc32_{d}", (DI, TOK)); dint(f"xcb_{d}", (DI, TOK), BF16)
        dint(f"sz32_{d}", (DI, TOK)); dint(f"delta_{d}", (DI, TOK))
        dint(f"dbc_{d}", (DR + 2 * DS, TOK)); dint(f"yb_{d}", (DI, TOK), BF16)
    dint("h32", (D, TOK)); dint("attn32", (D, TOK)); dint("attnb", (D, TOK), BF16)
    dint("gates", (E, TOK)); dint("ffn32", (D, TOK)); dint("logits_dbg", (E, TOK))
    din("ws", (D, E)); din("bs", (E, 1))
    dout("x1", (D, TOK)); dout("x1b", (D, TOK), BF16)
    dout("xn1_32", (D, TOK)); dout("xn1b", (D, TOK), BF16)
    dout("probs1", (E, TOK))

    st = contextlib.ExitStack()
    ctx = Ctx(nc, st)
    limit = PHASE_LIMIT if DEBUG_LAUNCH == 0 else 99
    phases = [
        lambda: phase_ln(ctx, block, ap, "xT", "xn32", "xnb", tag="0"),
        lambda: phase_mamba_proj(ctx, block, ap, "f"),
        lambda: phase_mamba_proj(ctx, block, ap, "b"),
        lambda: phase_mamba_scan(ctx, block, ap, "f"),
        lambda: phase_mamba_scan(ctx, block, ap, "b"),
        lambda: phase_outproj_combine(ctx, block, ap),
        lambda: phase_l0_ffn(ctx, block, ap),
        lambda: phase_residual3(ctx, block, ap, "xT", "attn32", "ffn32", "x1", "x1b"),
        lambda: phase_ln(ctx, block, ap, "x1", "xn1_32", "xn1b", tag="1"),
        lambda: phase_router_probs(ctx, block, ap),
    ]
    with st, nc.Block() as block:
        for i, ph in enumerate(phases):
            if i < limit:
                ph()
    return nc



def kernel(**inputs):
    w = prep_shared(inputs)
    xs = shard_input(inputs["L0_attn_x"])

    nc0 = build_L0(dbg=DEBUG_TAPS if DEBUG_LAUNCH == 0 else ())
    names0 = ["xT", "lnw", "lnb", "gwT", "w1e", "w3e", "w2e", "w1s", "w3s", "w2s",
              "ones1x128", "ones128x1", "identf", "ws", "bs"]
    for d in ("f", "b"):
        names0 += [f"ipT_{d}", f"xpT_{d}", f"dtwT_{d}", f"opT_{d}", f"convw_{d}",
                   f"convb_{d}", f"dtb_{d}", f"Alog_{d}", f"Dsk_{d}"]
    in_maps = []
    for c in range(NCORES):
        m = {}
        for nm in names0:
            m[nm] = xs[c] if nm == "xT" else w[nm]
        in_maps.append(m)
    res0 = run_bass_kernel_spmd(nc0, in_maps, core_ids=list(range(NCORES)))
    if DEBUG_TAPS and DEBUG_LAUNCH == 0:
        return res0

    probs_full = np.concatenate([res0.results[c]["probs1"] for c in range(NCORES)], axis=1)
    nc1 = build_L1(dbg=DEBUG_TAPS if DEBUG_LAUNCH == 1 else ())
    names1 = ["wqkv", "bqkv", "wproj", "bproj", "w1L1", "b1L1", "w2L1", "b2L1",
              "cosT", "sinT", "identf", "identb", "causal", "ones1x128", "ones128x1"]
    in_maps1 = []
    for c in range(NCORES):
        m = {nm: w[nm] for nm in names1}
        m["x1"] = res0.results[c]["x1"]
        m["xn1_32"] = res0.results[c]["xn1_32"]
        m["xn1b"] = res0.results[c]["xn1b"]
        m["probsF"] = np.ascontiguousarray(np.roll(probs_full, -c * TOK, axis=1))
        in_maps1.append(m)
    res1 = run_bass_kernel_spmd(nc1, in_maps1, core_ids=list(range(NCORES)))
    if DEBUG_TAPS and DEBUG_LAUNCH == 1:
        return res1
    outs = []
    for c in range(NCORES):
        x2 = res1.results[c]["x2"]
        outs.append(np.ascontiguousarray(x2.T).reshape(B_LOC, T, D))
    return np.concatenate(outs, axis=0)
